# version 01 (final)
# baseline (speedup 1.0000x reference)
import contextlib

import numpy as np
import concourse.bass as bass
import concourse.mybir as mybir
from concourse.bass_utils import run_bass_kernel_spmd

F32 = mybir.dt.float32
BF16 = mybir.dt.bfloat16
AF = mybir.ActivationFunctionType
ALU = mybir.AluOpType
AX = mybir.AxisListType

NC = 8
B = 2
T = 8192
D = 1024
NTOK = B * T
TPC = NTOK // NC
NT = TPC // 128
H16 = 16
HD = 64


class Em:

    def __init__(self, nc, sem):
        self.nc = nc
        self.sem = sem
        self.c = 0

    def op(self, eng, meth, *a, dma=False, **kw):
        e = getattr(self.nc, eng)
        e.wait_ge(self.sem, self.c)
        ins = getattr(e, meth)(*a, **kw)
        step = 16 if dma else 1
        ins.then_inc(self.sem, step)
        self.c += step

    def mm(self, out, lhsT, rhs, start, stop):
        e = self.nc.tensor
        e.wait_ge(self.sem, self.c)
        ins = e.matmul(out, lhsT, rhs, start=start, stop=stop)
        ins.then_inc(self.sem, 1)
        self.c += 1

    def fin(self):
        self.nc.sync.wait_ge(self.sem, self.c)


class P:

    def __init__(self):
        self.nc = bass.Bass("TRN2", target_bir_lowering=False)
        self.ctx = contextlib.ExitStack()

    def inp(self, name, shape):
        return self.nc.dram_tensor(name, shape, F32, kind="ExternalInput").ap()

    def inpb(self, name, shape):
        return self.nc.dram_tensor(name, shape, BF16, kind="ExternalInput").ap()

    def out(self, name, shape, dt=F32):
        return self.nc.dram_tensor(name, shape, dt, kind="ExternalOutput").ap()

    def sb(self, name, shape, dt=F32):
        return self.ctx.enter_context(self.nc.sbuf_tensor(name, shape, dt))

    def ps(self, name, shape, dt=F32):
        return self.ctx.enter_context(self.nc.psum_tensor(name, shape, dt))

    def start(self):
        self.sem = self.ctx.enter_context(self.nc.semaphore("gsem"))
        self.blk = self.ctx.enter_context(self.nc.Block())
        self.em = Em(self.nc, self.sem)
        return self.em

    def close(self):
        self.em.fin()
        self.ctx.close()
        return self.nc


def _bcast_row(em, psum_bc, row_ap, out_ap, ones_row, O):
    for oc in range(0, O, 512):
        w = min(512, O - oc)
        em.mm(psum_bc[:, 0:w], ones_row[0:1, :], row_ap[0:1, oc:oc + w], True, True)
        em.op("vector", "tensor_copy", out=out_ap[:, oc:oc + w], in_=psum_bc[:, 0:w])


def _load_cast(em, wt_d, stage, dst, rows_total, cols, dst_off=0):
    nrt = (rows_total + 127) // 128
    for j in range(nrt):
        r0 = j * 128
        nr = min(rows_total, r0 + 128) - r0
        em.op("sync", "dma_start", dma=True,
              out=stage[0:nr, 0:cols], in_=wt_d[r0:r0 + nr, :])
        em.op("vector", "tensor_copy",
              out=dst[0:nr, dst_off + j * cols: dst_off + (j + 1) * cols],
              in_=stage[0:nr, 0:cols])


def _ln_tile(em, out_ap, x_ap, sq, scr, eps, width=D):
    em.op("vector", "reduce_sum", out=scr[:, 0:1], in_=x_ap, axis=AX.X)
    em.op("scalar", "mul", out=scr[:, 0:1], in_=scr[:, 0:1], mul=1.0 / width)
    em.op("vector", "tensor_scalar", out=out_ap, in0=x_ap, scalar1=scr[:, 0:1],
          scalar2=None, op0=ALU.subtract)
    em.op("scalar", "activation", out=sq, in_=out_ap, func=AF.Square)
    em.op("vector", "reduce_sum", out=scr[:, 1:2], in_=sq, axis=AX.X)
    em.op("vector", "tensor_scalar", out=scr[:, 1:2], in0=scr[:, 1:2],
          scalar1=1.0 / width, scalar2=eps, op0=ALU.mult, op1=ALU.add)
    em.op("scalar", "activation", out=scr[:, 2:3], in_=scr[:, 1:2], func=AF.Sqrt)
    em.op("vector", "reciprocal", out=scr[:, 2:3], in_=scr[:, 2:3])
    em.op("vector", "tensor_scalar", out=out_ap, in0=out_ap, scalar1=scr[:, 2:3],
          scalar2=None, op0=ALU.mult)


def _tr(em, dst_ap, src_ap, psum_tr, eye):
    p, f = src_ap.shape[0], src_ap.shape[1]
    em.op("tensor", "transpose", out=psum_tr[0:f, 0:p], in_=src_ap,
          identity=eye[0:p, 0:p])
    em.op("vector", "tensor_copy", out=dst_ap, in_=psum_tr[0:f, 0:p])


def build_p1():
    p = P()
    xs_d = p.inp("xs", [TPC, D])
    wqkvT_d = p.inp("wqkvT", [D, 3 * D])
    bqkv_d = p.inp("bqkv", [1, 3 * D])
    wg1T_d = p.inp("wg1T", [D, HD])
    bg1_d = p.inp("bg1", [1, HD])
    wg2T_d = p.inp("wg2T", [HD, D])
    bg2_d = p.inp("bg2", [1, D])
    eye_d = p.inp("eye", [128, 128])
    ones_d = p.inp("ones_row", [1, 128])
    qs_o = p.out("qs", [TPC, D])
    gate_o = p.out("gate", [TPC, D])
    kvp_o = p.out("kvp", [HD, H16 * 65])

    x_sb = p.sb("x_sb", [128, D])
    stage = p.sb("stage", [128, 3 * D])
    wq_sb = p.sb("wq_sb", [128, 8 * 3 * D], BF16)
    wg1_sb = p.sb("wg1_sb", [128, 8 * HD], BF16)
    wg2_sb = p.sb("wg2_sb", [64, D], BF16)
    rows = p.sb("rows", [1, 3 * D + HD + D + 128])
    bqkv_bc = p.sb("bqkv_bc", [128, 3 * D])
    bg1_bc = p.sb("bg1_bc", [128, HD])
    bg2_bc = p.sb("bg2_bc", [128, D])
    eye_sb = p.sb("eye_sb", [128, 128])
    eye_bf = p.sb("eye_bf", [128, 128], BF16)
    onesb = p.sb("onesb", [128, 1], BF16)
    ln_sb = p.sb("ln_sb", [128, D])
    lnT_sb = p.sb("lnT_sb", [128, 8 * 128], BF16)
    sq = p.sb("sq", [128, D])
    scr = p.sb("scr", [128, 4])
    qkv_sb = p.sb("qkv_sb", [128, 3 * D])
    tmp = p.sb("tmp", [128, D])
    ek_bf = p.sb("ek_bf", [128, NT * D], BF16)
    v_bf = p.sb("v_bf", [128, NT * D], BF16)
    gh_bf = p.sb("gh_bf", [128, 64], BF16)
    ghT_bf = p.sb("ghT_bf", [64, 128], BF16)
    kvp_sb = p.sb("kvp_sb", [64, H16 * 65])
    ps_a = p.ps("ps_a", [128, 512])
    ps_tr = p.ps("ps_tr", [128, 128])
    ps_trb = p.ps("ps_trb", [128, 128], BF16)
    ps_kv = p.ps("ps_kv", [64, H16 * 64])
    ps_se = p.ps("ps_se", [64, H16])
    ps_g = p.ps("ps_g", [128, 64])
    ps_gate = p.ps("ps_gate", [128, 512])

    em = p.start()
    em.op("sync", "dma_start", dma=True, out=eye_sb[:], in_=eye_d)
    em.op("vector", "tensor_copy", out=eye_bf[:], in_=eye_sb[:])
    em.op("sync", "dma_start", dma=True, out=rows[0:1, 0:3 * D], in_=bqkv_d)
    em.op("sync", "dma_start", dma=True, out=rows[0:1, 3 * D:3 * D + HD], in_=bg1_d)
    em.op("sync", "dma_start", dma=True, out=rows[0:1, 3 * D + HD:4 * D + HD], in_=bg2_d)
    em.op("sync", "dma_start", dma=True, out=rows[0:1, 4 * D + HD:4 * D + HD + 128], in_=ones_d)
    ones_row = rows[0:1, 4 * D + HD:4 * D + HD + 128]
    em.op("vector", "memset", onesb[:], 1.0)
    _load_cast(em, wqkvT_d, stage, wq_sb, D, 3 * D)
    _load_cast(em, wg1T_d, stage, wg1_sb, D, HD)
    em.op("sync", "dma_start", dma=True, out=stage[0:64, 0:D], in_=wg2T_d)
    em.op("vector", "tensor_copy", out=wg2_sb[:], in_=stage[0:64, 0:D])
    _bcast_row(em, ps_a, rows[0:1, 0:3 * D], bqkv_bc, ones_row, 3 * D)
    _bcast_row(em, ps_a, rows[0:1, 3 * D:3 * D + HD], bg1_bc, ones_row, HD)
    _bcast_row(em, ps_a, rows[0:1, 3 * D + HD:4 * D + HD], bg2_bc, ones_row, D)
    for i in range(NT):
        em.op("sync", "dma_start", dma=True, out=x_sb[:], in_=xs_d[i * 128:(i + 1) * 128, :])
        _ln_tile(em, ln_sb[:], x_sb[:], sq[:], scr, 1e-5)
        for j in range(8):
            _tr(em, lnT_sb[:, j * 128:(j + 1) * 128], ln_sb[:, j * 128:(j + 1) * 128], ps_tr, eye_sb)
        for oc in range(6):
            for j in range(8):
                em.mm(ps_a[:, :], lnT_sb[:, j * 128:(j + 1) * 128],
                      wq_sb[:, j * 3 * D + oc * 512:j * 3 * D + (oc + 1) * 512], j == 0, j == 7)
            em.op("vector", "tensor_add", out=qkv_sb[:, oc * 512:(oc + 1) * 512],
                  in0=ps_a[:, :], in1=bqkv_bc[:, oc * 512:(oc + 1) * 512])
        em.op("scalar", "activation", out=tmp[:], in_=qkv_sb[:, 0:D], func=AF.Silu)
        em.op("sync", "dma_start", dma=True, out=qs_o[i * 128:(i + 1) * 128, :], in_=tmp[:])
        em.op("scalar", "activation", out=ek_bf[:, i * D:(i + 1) * D], in_=qkv_sb[:, D:2 * D], func=AF.Exp)
        em.op("vector", "tensor_copy", out=v_bf[:, i * D:(i + 1) * D], in_=qkv_sb[:, 2 * D:3 * D])
        for j in range(8):
            em.mm(ps_g[:, :], lnT_sb[:, j * 128:(j + 1) * 128],
                  wg1_sb[:, j * HD:(j + 1) * HD], j == 0, j == 7)
        em.op("vector", "tensor_add", out=tmp[:, 0:64], in0=ps_g[:, :], in1=bg1_bc[:, :])
        em.op("vector", "tensor_copy", out=gh_bf[:, :], in_=tmp[:, 0:64])
        _tr(em, ghT_bf[:, :], gh_bf[:, :], ps_trb, eye_bf)
        for oc in range(2):
            em.mm(ps_gate[:, :], ghT_bf[:, :], wg2_sb[:, oc * 512:(oc + 1) * 512], True, True)
            em.op("vector", "tensor_add", out=tmp[:, oc * 512:(oc + 1) * 512],
                  in0=ps_gate[:, :], in1=bg2_bc[:, oc * 512:(oc + 1) * 512])
        em.op("scalar", "activation", out=tmp[:], in_=tmp[:], func=AF.Sigmoid)
        em.op("sync", "dma_start", dma=True, out=gate_o[i * 128:(i + 1) * 128, :], in_=tmp[:])
    for h in range(H16):
        for i in range(NT):
            em.mm(ps_kv[:, h * 64:(h + 1) * 64], ek_bf[:, i * D + h * 64:i * D + (h + 1) * 64],
                  v_bf[:, i * D + h * 64:i * D + (h + 1) * 64], i == 0, i == NT - 1)
        for i in range(NT):
            em.mm(ps_se[:, h:h + 1], ek_bf[:, i * D + h * 64:i * D + (h + 1) * 64],
                  onesb[:], i == 0, i == NT - 1)
        em.op("vector", "tensor_copy", out=kvp_sb[:, h * 65:h * 65 + 64], in_=ps_kv[:, h * 64:(h + 1) * 64])
        em.op("vector", "tensor_copy", out=kvp_sb[:, h * 65 + 64:h * 65 + 65], in_=ps_se[:, h:h + 1])
    em.op("sync", "dma_start", dma=True, out=kvp_o, in_=kvp_sb[:])
    return p.close()


def build_p2():
    p = P()
    xs_d = p.inp("xs", [TPC, D])
    qsT_d = p.inp("qsT", [D, TPC])
    gate_d = p.inp("gate", [TPC, D])
    kvp_d = p.inp("kvp4", [4, HD, H16 * 65])
    woT_d = p.inp("woT", [D, D])
    bo_d = p.inp("bo", [1, D])
    lng_d = p.inp("lng", [1, D])
    lnb_d = p.inp("lnb", [1, D])
    eye_d = p.inp("eye", [128, 128])
    ones_d = p.inp("ones_row", [1, 128])
    xo = p.out("xo", [TPC, D])

    x_sb = p.sb("x_sb", [128, D])
    gate_sb = p.sb("gate_sb", [128, D])
    qsT_sb = p.sb("qsT_sb", [64, 16 * 128], BF16)
    stage = p.sb("stage", [128, D])
    kvp_sb = p.sb("kvp_sb", [64, 4 * H16 * 65])
    kvn_bf = p.sb("kvn_bf", [64, H16 * 64], BF16)
    rse = p.sb("rse", [64, 2])
    wo_sb = p.sb("wo_sb", [128, 8 * D], BF16)
    rows = p.sb("rows", [1, 3 * D + 128])
    bo_bc = p.sb("bo_bc", [128, D])
    lng_bc = p.sb("lng_bc", [128, D])
    lnb_bc = p.sb("lnb_bc", [128, D])
    eye_sb = p.sb("eye_sb", [128, 128])
    eye_bf = p.sb("eye_bf", [128, 128], BF16)
    att = p.sb("att", [128, D])
    sq = p.sb("sq", [128, D])
    scr = p.sb("scr", [128, 4])
    att_bf = p.sb("att_bf", [128, D], BF16)
    attT_bf = p.sb("attT_bf", [128, 8 * 128], BF16)
    tmp = p.sb("tmp", [128, D])
    ps_a = p.ps("ps_a", [128, 512])
    ps_tr = p.ps("ps_tr", [128, 128])
    ps_trb = p.ps("ps_trb", [128, 128], BF16)
    ps_att = p.ps("ps_att", [128, 1024])

    em = p.start()
    em.op("sync", "dma_start", dma=True, out=eye_sb[:], in_=eye_d)
    em.op("vector", "tensor_copy", out=eye_bf[:], in_=eye_sb[:])
    em.op("sync", "dma_start", dma=True, out=rows[0:1, 0:D], in_=bo_d)
    em.op("sync", "dma_start", dma=True, out=rows[0:1, D:2 * D], in_=lng_d)
    em.op("sync", "dma_start", dma=True, out=rows[0:1, 2 * D:3 * D], in_=lnb_d)
    em.op("sync", "dma_start", dma=True, out=rows[0:1, 3 * D:3 * D + 128], in_=ones_d)
    ones_row = rows[0:1, 3 * D:3 * D + 128]
    for c in range(4):
        em.op("sync", "dma_start", dma=True,
              out=kvp_sb[:, c * H16 * 65:(c + 1) * H16 * 65], in_=kvp_d[c])
    for c in range(1, 4):
        em.op("vector", "tensor_add", out=kvp_sb[:, 0:H16 * 65],
              in0=kvp_sb[:, 0:H16 * 65], in1=kvp_sb[:, c * H16 * 65:(c + 1) * H16 * 65])
    for h in range(H16):
        em.op("vector", "reciprocal", out=rse[:, 0:1], in_=kvp_sb[:, h * 65 + 64:h * 65 + 65])
        em.op("vector", "tensor_scalar", out=kvn_bf[:, h * 64:(h + 1) * 64],
              in0=kvp_sb[:, h * 65:h * 65 + 64], scalar1=rse[:, 0:1], scalar2=None, op0=ALU.mult)
    _load_cast(em, woT_d, stage, wo_sb, D, D)
    _bcast_row(em, ps_a, rows[0:1, 0:D], bo_bc, ones_row, D)
    _bcast_row(em, ps_a, rows[0:1, D:2 * D], lng_bc, ones_row, D)
    _bcast_row(em, ps_a, rows[0:1, 2 * D:3 * D], lnb_bc, ones_row, D)
    for i in range(NT):
        em.op("sync", "dma_start", dma=True, out=x_sb[:], in_=xs_d[i * 128:(i + 1) * 128, :])
        em.op("sync", "dma_start", dma=True, out=gate_sb[:], in_=gate_d[i * 128:(i + 1) * 128, :])
        for h in range(H16):
            em.op("sync", "dma_start", dma=True, out=stage[0:64, 0:128],
                  in_=qsT_d[h * 64:(h + 1) * 64, i * 128:(i + 1) * 128])
            em.op("vector", "tensor_copy", out=qsT_sb[:, h * 128:(h + 1) * 128], in_=stage[0:64, 0:128])
        for h in range(H16):
            em.mm(ps_att[:, h * 64:(h + 1) * 64],
                  qsT_sb[:, h * 128:(h + 1) * 128],
                  kvn_bf[:, h * 64:(h + 1) * 64], True, True)
        em.op("vector", "tensor_copy", out=att[:], in_=ps_att[:, :])
        _ln_tile(em, att[:], att[:], sq[:], scr, 1e-5)
        em.op("vector", "tensor_mul", out=att[:], in0=att[:], in1=lng_bc[:])
        em.op("vector", "tensor_add", out=att[:], in0=att[:], in1=lnb_bc[:])
        em.op("vector", "tensor_mul", out=att[:], in0=att[:], in1=gate_sb[:])
        em.op("vector", "tensor_copy", out=att_bf[:], in_=att[:])
        for j in range(8):
            _tr(em, attT_bf[:, j * 128:(j + 1) * 128], att_bf[:, j * 128:(j + 1) * 128],
                ps_trb, eye_bf)
        for oc in range(2):
            for j in range(8):
                em.mm(ps_a[:, :], attT_bf[:, j * 128:(j + 1) * 128],
                      wo_sb[:, j * D + oc * 512:j * D + (oc + 1) * 512], j == 0, j == 7)
            em.op("vector", "tensor_add", out=tmp[:, oc * 512:(oc + 1) * 512],
                  in0=ps_a[:, :], in1=bo_bc[:, oc * 512:(oc + 1) * 512])
        em.op("vector", "tensor_add", out=tmp[:], in0=tmp[:], in1=x_sb[:])
        em.op("sync", "dma_start", dma=True, out=xo[i * 128:(i + 1) * 128, :], in_=tmp[:])
    return p.close()


_NC_CACHE = {}


def _get(name, builder):
    if name not in _NC_CACHE:
        _NC_CACHE[name] = builder()
    return _NC_CACHE[name]


def _run(name, builder, in_maps):
    nc = _get(name, builder)
    return run_bass_kernel_spmd(nc, in_maps, core_ids=list(range(NC))).results


_EYE = np.eye(128, dtype=np.float32)
_ONES = np.ones((1, 128), dtype=np.float32)


def _ct(a):
    return np.ascontiguousarray(a, dtype=np.float32)


def _stage_l0aa(X, inp):
    w_qkvT = _ct(inp["L0_attn_a_w_qkv"].T)
    bqkv = _ct(inp["L0_attn_a_b_qkv"][None])
    wg1T = _ct(inp["L0_attn_a_w_g1"].T)
    bg1 = _ct(inp["L0_attn_a_b_g1"][None])
    wg2T = _ct(inp["L0_attn_a_w_g2"].T)
    bg2 = _ct(inp["L0_attn_a_b_g2"][None])
    im = [dict(xs=_ct(X[c * TPC:(c + 1) * TPC]), wqkvT=w_qkvT, bqkv=bqkv,
               wg1T=wg1T, bg1=bg1, wg2T=wg2T, bg2=bg2, eye=_EYE, ones_row=_ONES)
          for c in range(NC)]
    r1 = _run("p1", build_p1, im)
    woT = _ct(inp["L0_attn_a_w_out"].T)
    bo = _ct(inp["L0_attn_a_b_out"][None])
    lng = _ct(inp["L0_attn_a_ln_g"][None])
    lnb = _ct(inp["L0_attn_a_ln_b"][None])
    im2 = []
    for c in range(NC):
        bgrp = (c // 4) * 4
        kvp4 = np.stack([np.asarray(r1[bgrp + j]["kvp"]) for j in range(4)])
        im2.append(dict(xs=_ct(X[c * TPC:(c + 1) * TPC]),
                        qsT=_ct(np.asarray(r1[c]["qs"]).T),
                        gate=_ct(np.asarray(r1[c]["gate"])),
                        kvp4=_ct(kvp4), woT=woT, bo=bo, lng=lng, lnb=lnb,
                        eye=_EYE, ones_row=_ONES))
    r2 = _run("p2", build_p2, im2)
    return np.concatenate([np.asarray(r2[c]["xo"]) for c in range(NC)], axis=0)


def kernel(**inp):
    X = _ct(inp["L0_attn_a_x"].reshape(NTOK, D))
    X = _stage_l0aa(X, inp)
    X = _stage_l0ffn(X, inp)
    X = _stage_l0ab(X, inp)
    X = _stage_l0ffn(X, inp)
    X = _stage_l1aa(X, inp)
    X = _stage_l1ffn(X, inp)
    X = _stage_l1ab(X, inp)
    X = _stage_l1ffn(X, inp)
    return X.reshape(B, T, D)


def build_p3():
    p = P()
    xs_d = p.inp("xs", [TPC, D])
    rw1_d = p.inp("rw1", [D, 128])
    rw2_d = p.inp("rw2", [128, 16])
    eye_d = p.inp("eye", [128, 128])
    lnx_o = p.out("lnx", [TPC, D], BF16)
    lgT_o = p.out("lgT", [16, TPC])

    x_sb = p.sb("x_sb", [128, D])
    stage = p.sb("stage", [128, 128])
    w1_sb = p.sb("w1_sb", [128, 8 * 128], BF16)
    w2_sb = p.sb("w2_sb", [128, 16], BF16)
    eye_sb = p.sb("eye_sb", [128, 128])
    eye_bf = p.sb("eye_bf", [128, 128], BF16)
    ln_sb = p.sb("ln_sb", [128, D])
    ln_bf = p.sb("ln_bf", [128, D], BF16)
    lnT_bf = p.sb("lnT_bf", [128, 8 * 128], BF16)
    sq = p.sb("sq", [128, D])
    scr = p.sb("scr", [128, 4])
    r1_bf = p.sb("r1_bf", [128, 128], BF16)
    r1T_bf = p.sb("r1T_bf", [128, 128], BF16)
    lg_sb = p.sb("lg_sb", [128, 16])
    lgT_sb = p.sb("lgT_sb", [16, TPC])
    ps_a = p.ps("ps_a", [128, 512])
    ps_tr = p.ps("ps_tr", [128, 128])
    ps_trb = p.ps("ps_trb", [128, 128], BF16)

    em = p.start()
    em.op("sync", "dma_start", dma=True, out=eye_sb[:], in_=eye_d)
    em.op("vector", "tensor_copy", out=eye_bf[:], in_=eye_sb[:])
    _load_cast(em, rw1_d, stage, w1_sb, D, 128)
    em.op("sync", "dma_start", dma=True, out=stage[:, 0:16], in_=rw2_d)
    em.op("vector", "tensor_copy", out=w2_sb[:], in_=stage[:, 0:16])
    for i in range(NT):
        em.op("sync", "dma_start", dma=True, out=x_sb[:], in_=xs_d[i * 128:(i + 1) * 128, :])
        _ln_tile(em, ln_sb[:], x_sb[:], sq[:], scr, 1e-5)
        em.op("vector", "tensor_copy", out=ln_bf[:], in_=ln_sb[:])
        em.op("sync", "dma_start", dma=True, out=lnx_o[i * 128:(i + 1) * 128, :], in_=ln_bf[:])
        for j in range(8):
            _tr(em, lnT_bf[:, j * 128:(j + 1) * 128], ln_bf[:, j * 128:(j + 1) * 128], ps_trb, eye_bf)
        for j in range(8):
            em.mm(ps_a[:, 0:128], lnT_bf[:, j * 128:(j + 1) * 128],
                  w1_sb[:, j * 128:(j + 1) * 128], j == 0, j == 7)
        em.op("scalar", "activation", out=r1_bf[:], in_=ps_a[:, 0:128], func=AF.Silu)
        _tr(em, r1T_bf[:], r1_bf[:], ps_trb, eye_bf)
        em.mm(ps_a[:, 0:16], r1T_bf[:], w2_sb[:], True, True)
        em.op("vector", "tensor_copy", out=lg_sb[:], in_=ps_a[:, 0:16])
        _tr(em, lgT_sb[:, i * 128:(i + 1) * 128], lg_sb[:], ps_tr, eye_sb)
    em.op("sync", "dma_start", dma=True, out=lgT_o, in_=lgT_sb[:])
    return p.close()


def build_p4():
    p = P()
    lnx_d = p.inpb("lnxf", [T, D])
    lgT_d = p.inp("lgTf", [16, T])
    w1_d = p.inp("w1e", [4, D, 2048])
    w2_d = p.inp("w2e", [4, D, 2048])
    w3_d = p.inp("w3e", [4, 2048, D])
    iota256_d = p.inp("iota256", [128, 256])
    eye_d = p.inp("eye", [128, 128])
    e0_d = p.inp("e0col", [1, 1])
    ow_o = p.out("ow", [4, 256, D])
    pos_o = p.out("pos", [16, T])

    A = p.sb("A", [16, T])
    W = p.sb("W", [16, T])
    C = p.sb("Cb", [16, T])
    m8 = p.sb("m8", [16, 8])
    scr = p.sb("scr", [16, 4])
    onec = p.sb("onec", [16, 1])
    stage = p.sb("stage", [128, 1024])
    iota256 = p.sb("iota256s", [128, 256])
    eye_sb = p.sb("eye_sb", [128, 128])
    eye_bf = p.sb("eye_bf", [128, 128], BF16)
    posT = p.sb("posT", [128, 64 * 16])
    wnT_bf = p.sb("wnT_bf", [128, 64 * 16], BF16)
    lnc_bf = p.sb("lnc_bf", [128, D], BF16)
    selT_bf = p.sb("selT_bf", [128, 256], BF16)
    w1_bf = p.sb("w1_bf", [128, 8 * 2048], BF16)
    w2_bf = p.sb("w2_bf", [128, 8 * 2048], BF16)
    xg_bf = p.sb("xg_bf", [128, 2 * D], BF16)
    xgT_bf = p.sb("xgT_bf", [128, 8 * 256], BF16)
    wc = p.sb("wc", [128, 2])
    h_bf = p.sb("h_bf", [128, 2 * 2048], BF16)
    hT_bf = p.sb("hT_bf", [128, 16 * 256], BF16)
    tmp = p.sb("tmp", [128, 512])
    ow_sb = p.sb("ow_sb", [128, D])
    ps_x0 = p.ps("ps_x0", [128, 1536])
    ps_x1 = p.ps("ps_x1", [128, 1536])
    ps_tr = p.ps("ps_tr", [128, 128])
    ps_trb = p.ps("ps_trb", [128, 128], BF16)

    em = p.start()
    em.op("sync", "dma_start", dma=True, out=eye_sb[:], in_=eye_d)
    em.op("vector", "tensor_copy", out=eye_bf[:], in_=eye_sb[:])
    em.op("sync", "dma_start", dma=True, out=iota256[:], in_=iota256_d)
    em.op("sync", "dma_start", dma=True, out=A[:], in_=lgT_d)
    em.op("scalar", "activation", out=A[:], in_=A[:], func=AF.Exp)
    em.op("vector", "reduce_sum", out=scr[:, 0:1], in_=A[:], axis=AX.X)
    em.op("vector", "reciprocal", out=scr[:, 0:1], in_=scr[:, 0:1])
    em.op("vector", "tensor_scalar", out=A[:], in0=A[:], scalar1=scr[:, 0:1],
          scalar2=None, op0=ALU.mult)
    em.op("vector", "tensor_copy", out=W[:], in_=A[:])
    for it in range(32):
        em.op("vector", "max", out=m8[:], in_=W[:])
        em.op("vector", "match_replace", out=W[:], in_to_replace=m8[:], in_values=W[:],
              imm_value=0.0)
    em.op("vector", "tensor_sub", out=A[:], in0=A[:], in1=W[:])
    em.op("vector", "reduce_sum", out=scr[:, 1:2], in_=A[:], axis=AX.X)
    em.op("vector", "tensor_scalar_add", out=scr[:, 1:2], in0=scr[:, 1:2], scalar1=1e-9)
    em.op("vector", "reciprocal", out=scr[:, 1:2], in_=scr[:, 1:2])
    em.op("vector", "tensor_scalar", out=W[:], in0=A[:], scalar1=0.0, scalar2=None,
          op0=ALU.is_gt)
    em.op("vector", "tensor_scalar", out=A[:], in0=A[:], scalar1=scr[:, 1:2],
          scalar2=None, op0=ALU.mult)
    em.op("vector", "memset", onec[:], 1.0)
    em.op("vector", "tensor_tensor_scan", out=C[:], data0=W[:], data1=onec[:].to_broadcast([16, T]),
          initial=0.0, op0=ALU.add, op1=ALU.mult)
    em.op("vector", "tensor_sub", out=C[:], in0=C[:], in1=W[:])
    em.op("vector", "tensor_scalar", out=W[:], in0=W[:], scalar1=-1e6, scalar2=1e6,
          op0=ALU.mult, op1=ALU.add)
    em.op("vector", "tensor_add", out=C[:], in0=C[:], in1=W[:])
    em.op("sync", "dma_start", dma=True, out=pos_o, in_=C[:])
    for i in range(64):
        _tr(em, posT[:, i * 16:(i + 1) * 16], C[:, i * 128:(i + 1) * 128], ps_tr, eye_sb)
        _tr(em, wnT_bf[:, i * 16:(i + 1) * 16], A[:, i * 128:(i + 1) * 128], ps_tr, eye_sb)
    for le in range(4):
        for j in range(8):
            for hf in range(2):
                em.op("sync", "dma_start", dma=True, out=stage[:, 0:1024],
                      in_=w1_d[le, j * 128:(j + 1) * 128, hf * 1024:(hf + 1) * 1024])
                em.op("vector", "tensor_copy", out=w1_bf[:, j * 2048 + hf * 1024:j * 2048 + (hf + 1) * 1024],
                      in_=stage[:, 0:1024])
                em.op("sync", "dma_start", dma=True, out=stage[:, 0:1024],
                      in_=w2_d[le, j * 128:(j + 1) * 128, hf * 1024:(hf + 1) * 1024])
                em.op("vector", "tensor_copy", out=w2_bf[:, j * 2048 + hf * 1024:j * 2048 + (hf + 1) * 1024],
                      in_=stage[:, 0:1024])
        for i in range(64):
            em.op("sync", "dma_start", dma=True, out=lnc_bf[:],
                  in_=lnx_d[i * 128:(i + 1) * 128, :])
            em.op("vector", "tensor_scalar", out=selT_bf[:], in0=iota256[:],
                  scalar1=posT[:, i * 16 + le:i * 16 + le + 1],
                  scalar2=None, op0=ALU.is_equal)
            for jb, px in ((0, ps_x0), (1, ps_x1)):
                lh = selT_bf[:, jb * 128:(jb + 1) * 128]
                em.mm(px[:, 0:512], lh, lnc_bf[:, 0:512], i == 0, i == 63)
                em.mm(px[:, 512:1024], lh, lnc_bf[:, 512:1024], i == 0, i == 63)
                em.mm(px[:, 1024:1025], lh, wnT_bf[:, i * 16 + le:i * 16 + le + 1], i == 0, i == 63)
        for jb, px in ((0, ps_x0), (1, ps_x1)):
            em.op("vector", "tensor_copy", out=xg_bf[:, jb * D:(jb + 1) * D], in_=px[:, 0:1024])
            em.op("vector", "tensor_copy", out=wc[:, jb:jb + 1], in_=px[:, 1024:1025])
        for jb in range(2):
            for db in range(8):
                _tr(em, xgT_bf[:, db * 256 + jb * 128:db * 256 + (jb + 1) * 128],
                    xg_bf[:, jb * D + db * 128:jb * D + (db + 1) * 128], ps_trb, eye_bf)
        for jb in range(2):
            for oc in range(4):
                for db in range(8):
                    lh = xgT_bf[:, db * 256 + jb * 128:db * 256 + (jb + 1) * 128]
                    em.mm(ps_x0[:, 0:512], lh, w1_bf[:, db * 2048 + oc * 512:db * 2048 + (oc + 1) * 512], db == 0, db == 7)
                for db in range(8):
                    lh = xgT_bf[:, db * 256 + jb * 128:db * 256 + (jb + 1) * 128]
                    em.mm(ps_x1[:, 0:512], lh, w2_bf[:, db * 2048 + oc * 512:db * 2048 + (oc + 1) * 512], db == 0, db == 7)
                em.op("scalar", "activation", out=tmp[:], in_=ps_x0[:, 0:512], func=AF.Silu)
                em.op("vector", "tensor_tensor", out=h_bf[:, jb * 2048 + oc * 512:jb * 2048 + (oc + 1) * 512],
                      in0=ps_x1[:, 0:512], in1=tmp[:], op=ALU.mult)
        _load_cast(em, w3_d[le], stage, w1_bf, 2048, D)
        for jb in range(2):
            for fb in range(16):
                _tr(em, hT_bf[:, fb * 256 + jb * 128:fb * 256 + (jb + 1) * 128],
                    h_bf[:, jb * 2048 + fb * 128:jb * 2048 + (fb + 1) * 128],
                    ps_trb, eye_bf)
        for jb in range(2):
            for oc in range(2):
                for fb in range(16):
                    lh = hT_bf[:, fb * 256 + jb * 128:fb * 256 + (jb + 1) * 128]
                    em.mm(ps_x0[:, 0:512], lh, w1_bf[:, fb * D + oc * 512:fb * D + (oc + 1) * 512], fb == 0, fb == 15)
                em.op("vector", "tensor_scalar", out=ow_sb[:, oc * 512:(oc + 1) * 512],
                      in0=ps_x0[:, 0:512], scalar1=wc[:, jb:jb + 1], scalar2=None, op0=ALU.mult)
            em.op("sync", "dma_start", dma=True, out=ow_o[le, jb * 128:(jb + 1) * 128, :],
                  in_=ow_sb[:])
    return p.close()


def build_p5():
    p = P()
    xs_d = p.inp("xs", [TPC, D])
    ow_d = p.inp("owall", [16, 256, D])
    pos_d = p.inp("posSlice", [16, TPC])
    iotaJ_d = p.inp("iotaJ", [128, 2])
    ones_d = p.inp("ones_row", [1, 128])
    eye_d = p.inp("eye", [128, 128])
    xo = p.out("xo", [TPC, D])

    ow_bf = p.sb("ow_bf", [128, 32 * D], BF16)
    stage = p.sb("stage", [128, D])
    posr = p.sb("posr", [1, 2048])
    iotaJ = p.sb("iotaJs", [128, 2])
    ones_sb = p.sb("ones_sb", [1, 128])
    eye_sb = p.sb("eye_sb", [128, 128])
    sel_bf = p.sb("sel_bf", [128, 32 * 512], BF16)
    yT_sb = p.sb("yT_sb", [128, 512])
    y_sb = p.sb("y_sb", [128, NT * D])
    x_sb = p.sb("x_sb", [128, D])
    ps_posb = p.ps("ps_posb", [128, 512])
    ps_y0 = p.ps("ps_y0", [128, 512])
    ps_y1 = p.ps("ps_y1", [128, 512])
    ps_y2 = p.ps("ps_y2", [128, 512])
    ps_y3 = p.ps("ps_y3", [128, 512])
    ps_tr = p.ps("ps_tr", [128, 128])

    em = p.start()
    ps_y = [ps_y0, ps_y1, ps_y2, ps_y3]
    em.op("sync", "dma_start", dma=True, out=eye_sb[:], in_=eye_d)
    em.op("sync", "dma_start", dma=True, out=iotaJ[:], in_=iotaJ_d)
    em.op("sync", "dma_start", dma=True, out=ones_sb[:], in_=ones_d)
    for e in range(16):
        for jb in range(2):
            em.op("sync", "dma_start", dma=True, out=stage[:],
                  in_=ow_d[e, jb * 128:(jb + 1) * 128, :])
            em.op("vector", "tensor_copy", out=ow_bf[:, (e * 2 + jb) * D:(e * 2 + jb + 1) * D],
                  in_=stage[:])
    for sc in range(4):
        for eg in range(4):
            for le in range(4):
                em.op("sync", "dma_start", dma=True, out=posr[0:1, le * 512:(le + 1) * 512],
                      in_=pos_d[eg * 4 + le:eg * 4 + le + 1, sc * 512:(sc + 1) * 512])
            for le in range(4):
                e = eg * 4 + le
                em.mm(ps_posb[:], ones_sb[0:1, :], posr[0:1, le * 512:(le + 1) * 512], True, True)
                for jb in range(2):
                    em.op("vector", "tensor_scalar", out=sel_bf[:, (e * 2 + jb) * 512:(e * 2 + jb + 1) * 512],
                          in0=ps_posb[:], scalar1=iotaJ[:, jb:jb + 1], scalar2=None, op0=ALU.is_equal)
        for dgrp in range(2):
            for db in range(4):
                d0 = (dgrp * 4 + db) * 128
                for e in range(16):
                    for jb in range(2):
                        em.mm(ps_y[db][:], ow_bf[:, (e * 2 + jb) * D + d0:(e * 2 + jb) * D + d0 + 128],
                              sel_bf[:, (e * 2 + jb) * 512:(e * 2 + jb + 1) * 512],
                              e == 0 and jb == 0, e == 15 and jb == 1)
            for db in range(4):
                d0 = (dgrp * 4 + db) * 128
                em.op("vector", "tensor_copy", out=yT_sb[:], in_=ps_y[db][:])
                for tb in range(4):
                    _tr(em, y_sb[:, (sc * 4 + tb) * D + d0:(sc * 4 + tb) * D + d0 + 128],
                        yT_sb[:, tb * 128:(tb + 1) * 128], ps_tr, eye_sb)
    for i in range(NT):
        em.op("sync", "dma_start", dma=True, out=x_sb[:], in_=xs_d[i * 128:(i + 1) * 128, :])
        em.op("vector", "tensor_add", out=x_sb[:], in0=x_sb[:], in1=y_sb[:, i * D:(i + 1) * D])
        em.op("sync", "dma_start", dma=True, out=xo[i * 128:(i + 1) * 128, :], in_=x_sb[:])
    return p.close()


def _stage_l0ffn(X, inp):
    rw1 = _ct(inp["L0_ffn_router_w1"])
    rw2 = _ct(inp["L0_ffn_router_w2"])
    im = [dict(xs=_ct(X[c * TPC:(c + 1) * TPC]), rw1=rw1, rw2=rw2, eye=_EYE)
          for c in range(NC)]
    r3 = _run("p3", build_p3, im)
    iota256 = np.tile(np.arange(256, dtype=np.float32), (128, 1))
    im4 = []
    for c in range(NC):
        b = c // 4
        e0 = 4 * (c % 4)
        lnxf = np.concatenate([np.asarray(r3[4 * b + j]["lnx"]) for j in range(4)], axis=0)
        lgTf = np.concatenate([np.asarray(r3[4 * b + j]["lgT"]) for j in range(4)], axis=1)
        perm = list(range(e0, e0 + 4)) + [e for e in range(16) if not (e0 <= e < e0 + 4)]
        lgTf = lgTf[perm]
        im4.append(dict(lnxf=lnxf, lgTf=_ct(lgTf),
                        w1e=_ct(inp["L0_ffn_w1"][e0:e0 + 4]),
                        w2e=_ct(inp["L0_ffn_w2"][e0:e0 + 4]),
                        w3e=_ct(inp["L0_ffn_w3"][e0:e0 + 4]),
                        iota256=iota256, eye=_EYE,
                        e0col=np.zeros((1, 1), np.float32)))
    r4 = _run("p4", build_p4, im4)
    iotaJ = np.stack([np.arange(128, dtype=np.float32),
                      np.arange(128, 256).astype(np.float32)], axis=1)
    im5 = []
    for c in range(NC):
        b = c // 4
        owall = np.concatenate([np.asarray(r4[4 * b + j]["ow"]) for j in range(4)], axis=0)
        pos = np.asarray(r4[4 * b]["pos"])[:, (c % 4) * TPC:(c % 4 + 1) * TPC]
        im5.append(dict(xs=_ct(X[c * TPC:(c + 1) * TPC]), owall=_ct(owall),
                        posSlice=_ct(pos), iotaJ=iotaJ, ones_row=_ONES, eye=_EYE))
    r5 = _run("p5", build_p5, im5)
    return np.concatenate([np.asarray(r5[c]["xo"]) for c in range(NC)], axis=0)


def build_p9():
    p = P()
    xs_d = p.inp("xs", [TPC, D])
    wqT_d = p.inp("wqT", [D, D])
    wkT_d = p.inp("wkT", [D, D])
    wvT_d = p.inp("wvT", [D, D])
    bq_d = p.inp("bq", [1, D])
    bk_d = p.inp("bk", [1, D])
    bv_d = p.inp("bv", [1, D])
    nq_d = p.inp("nq", [1, D])
    nk_d = p.inp("nk", [1, D])
    eye_d = p.inp("eye", [128, 128])
    ones_d = p.inp("ones_row", [1, 128])
    q_o = p.out("q", [TPC, D])
    k_o = p.out("k", [TPC, D])
    v_o = p.out("v", [TPC, D])

    x_sb = p.sb("x_sb", [128, D])
    stage = p.sb("stage", [128, D])
    wq_sb = p.sb("wq_sb", [128, 8 * D], BF16)
    wk_sb = p.sb("wk_sb", [128, 8 * D], BF16)
    wv_sb = p.sb("wv_sb", [128, 8 * D], BF16)
    rows = p.sb("rows", [1, 5 * D + 128])
    bc = p.sb("bc", [128, 5 * D])
    eye_sb = p.sb("eye_sb", [128, 128])
    ln_sb = p.sb("ln_sb", [128, D])
    ln_bf = p.sb("ln_bf", [128, D], BF16)
    lnT_bf = p.sb("lnT_bf", [128, 8 * 128], BF16)
    sq = p.sb("sq", [128, D])
    scr = p.sb("scr", [128, 4])
    t0 = p.sb("t0", [128, D])
    eye_bf = p.sb("eye_bf", [128, 128], BF16)
    ps_a = p.ps("ps_a", [128, 512])
    ps_trb = p.ps("ps_trb", [128, 128], BF16)

    em = p.start()
    em.op("sync", "dma_start", dma=True, out=eye_sb[:], in_=eye_d)
    em.op("vector", "tensor_copy", out=eye_bf[:], in_=eye_sb[:])
    for idx, dd in enumerate([bq_d, bk_d, bv_d, nq_d, nk_d]):
        em.op("sync", "dma_start", dma=True, out=rows[0:1, idx * D:(idx + 1) * D], in_=dd)
    em.op("sync", "dma_start", dma=True, out=rows[0:1, 5 * D:5 * D + 128], in_=ones_d)
    ones_row = rows[0:1, 5 * D:5 * D + 128]
    _load_cast(em, wqT_d, stage, wq_sb, D, D)
    _load_cast(em, wkT_d, stage, wk_sb, D, D)
    _load_cast(em, wvT_d, stage, wv_sb, D, D)
    _bcast_row(em, ps_a, rows[0:1, 0:5 * D], bc, ones_row, 5 * D)
    em.op("vector", "tensor_scalar_mul", out=bc[:, 3 * D:4 * D], in0=bc[:, 3 * D:4 * D], scalar1=0.125)
    for i in range(NT):
        em.op("sync", "dma_start", dma=True, out=x_sb[:], in_=xs_d[i * 128:(i + 1) * 128, :])
        _ln_tile(em, ln_sb[:], x_sb[:], sq[:], scr, 1e-5)
        em.op("vector", "tensor_copy", out=ln_bf[:], in_=ln_sb[:])
        for j in range(8):
            _tr(em, lnT_bf[:, j * 128:(j + 1) * 128], ln_bf[:, j * 128:(j + 1) * 128], ps_trb, eye_bf)
        for which, (w_sb, bof, out_d, nrm) in enumerate([(wq_sb, 0, q_o, 3), (wk_sb, D, k_o, 4), (wv_sb, 2 * D, v_o, None)]):
            for oc in range(2):
                for j in range(8):
                    em.mm(ps_a[:, :], lnT_bf[:, j * 128:(j + 1) * 128],
                          w_sb[:, j * D + oc * 512:j * D + (oc + 1) * 512], j == 0, j == 7)
                em.op("vector", "tensor_add", out=t0[:, oc * 512:(oc + 1) * 512],
                      in0=ps_a[:, :], in1=bc[:, bof + oc * 512:bof + (oc + 1) * 512])
            if nrm is not None:
                em.op("scalar", "activation", out=sq[:], in_=t0[:], func=AF.Square)
                em.op("vector", "reduce_sum", out=scr[:, 0:1], in_=sq[:], axis=AX.X)
                em.op("vector", "tensor_scalar", out=scr[:, 0:1], in0=scr[:, 0:1],
                      scalar1=1.0 / D, scalar2=1e-6, op0=ALU.mult, op1=ALU.add)
                em.op("scalar", "activation", out=scr[:, 1:2], in_=scr[:, 0:1], func=AF.Sqrt)
                em.op("vector", "reciprocal", out=scr[:, 1:2], in_=scr[:, 1:2])
                em.op("vector", "tensor_scalar", out=t0[:], in0=t0[:], scalar1=scr[:, 1:2],
                      scalar2=None, op0=ALU.mult)
                em.op("vector", "tensor_mul", out=t0[:], in0=t0[:], in1=bc[:, nrm * D:(nrm + 1) * D])
            em.op("sync", "dma_start", dma=True, out=out_d[i * 128:(i + 1) * 128, :], in_=t0[:])
    return p.close()


def build_p10():
    p = P()
    qT_d = p.inp("qT", [HD, T])
    kT_d = p.inp("kT", [HD, T])
    v_d = p.inp("vv", [T, HD])
    eye_d = p.inp("eye", [128, 128])
    ones_d = p.inp("ones_row", [1, 128])
    o_o = p.out("o", [T, HD])

    stage = p.sb("stage", [128, 2048])
    qT_bf = p.sb("qT_bf", [64, T], BF16)
    kT_bf = p.sb("kT_bf", [64, T], BF16)
    v_bf = p.sb("v_bf", [128, 64 * HD], BF16)
    onesc = p.sb("onesc", [128, 1], BF16)
    ones_sb = p.sb("ones_sb", [1, 128])
    eye_sb = p.sb("eye_sb", [128, 128])
    es_bf = p.sb("es_bf", [128, 512], BF16)
    rb = p.sb("rb", [64, 512])
    outT = p.sb("outT", [64, 512])
    otile = p.sb("otile", [128, HD])
    rs = p.sb("rs", [1, 512])
    ps_s = p.ps("ps_s", [128, 512])
    ps_o = p.ps("ps_o", [64, 512])
    ps_sum = p.ps("ps_sum", [1, 512])
    ps_b = p.ps("ps_b", [64, 512])
    ps_tr = p.ps("ps_tr", [128, 128])

    em = p.start()
    em.op("sync", "dma_start", dma=True, out=eye_sb[:], in_=eye_d)
    em.op("sync", "dma_start", dma=True, out=ones_sb[:], in_=ones_d)
    em.op("vector", "memset", onesc[:], 1.0)
    for cc in range(4):
        em.op("sync", "dma_start", dma=True, out=stage[0:64, :], in_=qT_d[:, cc * 2048:(cc + 1) * 2048])
        em.op("vector", "tensor_copy", out=qT_bf[:, cc * 2048:(cc + 1) * 2048], in_=stage[0:64, :])
        em.op("sync", "dma_start", dma=True, out=stage[0:64, :], in_=kT_d[:, cc * 2048:(cc + 1) * 2048])
        em.op("vector", "tensor_copy", out=kT_bf[:, cc * 2048:(cc + 1) * 2048], in_=stage[0:64, :])
    for it in range(4):
        em.op("sync", "dma_start", dma=True,
              out=stage[:, 0:1024].rearrange("p (a d) -> p a d", a=16),
              in_=v_d[it * 2048:(it + 1) * 2048, :].rearrange("(a p) d -> p a d", p=128))
        em.op("vector", "tensor_copy", out=v_bf[:, it * 1024:(it + 1) * 1024], in_=stage[:, 0:1024])
    for qc in range(16):
        for kt in range(64):
            em.mm(ps_s[:], kT_bf[:, kt * 128:(kt + 1) * 128],
                  qT_bf[:, qc * 512:(qc + 1) * 512], True, True)
            em.op("scalar", "activation", out=es_bf[:], in_=ps_s[:], func=AF.Exp)
            em.mm(ps_sum[:], onesc[:], es_bf[:], kt == 0, kt == 63)
            em.mm(ps_o[:], v_bf[:, kt * 64:(kt + 1) * 64], es_bf[:], kt == 0, kt == 63)
        em.op("vector", "tensor_copy", out=rs[:], in_=ps_sum[:])
        em.op("vector", "reciprocal", out=rs[:], in_=rs[:])
        em.mm(ps_b[:], ones_sb[0:1, 0:64], rs[0:1, :], True, True)
        em.op("vector", "tensor_copy", out=rb[:], in_=ps_b[:])
        em.op("vector", "tensor_mul", out=outT[:], in0=rb[:], in1=ps_o[:])
        for tb in range(4):
            _tr(em, otile[:, :], outT[:, tb * 128:(tb + 1) * 128], ps_tr, eye_sb)
            em.op("sync", "dma_start", dma=True,
                  out=o_o[qc * 512 + tb * 128:qc * 512 + (tb + 1) * 128, :], in_=otile[:])
    return p.close()


def build_p11():
    p = P()
    xs_d = p.inp("xs", [TPC, D])
    att_d = p.inp("att", [TPC, D])
    woT_d = p.inp("woT", [D, D])
    bo_d = p.inp("bo", [1, D])
    eye_d = p.inp("eye", [128, 128])
    ones_d = p.inp("ones_row", [1, 128])
    xo = p.out("xo", [TPC, D])

    x_sb = p.sb("x_sb", [128, D])
    a_sb = p.sb("a_sb", [128, D])
    a_bf = p.sb("a_bf", [128, D], BF16)
    aT_bf = p.sb("aT_bf", [128, 8 * 128], BF16)
    stage = p.sb("stage", [128, D])
    wo_sb = p.sb("wo_sb", [128, 8 * D], BF16)
    rows = p.sb("rows", [1, D + 128])
    bo_bc = p.sb("bo_bc", [128, D])
    eye_sb = p.sb("eye_sb", [128, 128])
    eye_bf = p.sb("eye_bf", [128, 128], BF16)
    tmp = p.sb("tmp", [128, D])
    ps_a = p.ps("ps_a", [128, 512])
    ps_trb = p.ps("ps_trb", [128, 128], BF16)

    em = p.start()
    em.op("sync", "dma_start", dma=True, out=eye_sb[:], in_=eye_d)
    em.op("vector", "tensor_copy", out=eye_bf[:], in_=eye_sb[:])
    em.op("sync", "dma_start", dma=True, out=rows[0:1, 0:D], in_=bo_d)
    em.op("sync", "dma_start", dma=True, out=rows[0:1, D:D + 128], in_=ones_d)
    _load_cast(em, woT_d, stage, wo_sb, D, D)
    _bcast_row(em, ps_a, rows[0:1, 0:D], bo_bc, rows[0:1, D:D + 128], D)
    for i in range(NT):
        em.op("sync", "dma_start", dma=True, out=x_sb[:], in_=xs_d[i * 128:(i + 1) * 128, :])
        em.op("sync", "dma_start", dma=True, out=a_sb[:], in_=att_d[i * 128:(i + 1) * 128, :])
        em.op("vector", "tensor_copy", out=a_bf[:], in_=a_sb[:])
        for j in range(8):
            _tr(em, aT_bf[:, j * 128:(j + 1) * 128], a_bf[:, j * 128:(j + 1) * 128], ps_trb, eye_bf)
        for oc in range(2):
            for j in range(8):
                em.mm(ps_a[:, :], aT_bf[:, j * 128:(j + 1) * 128],
                      wo_sb[:, j * D + oc * 512:j * D + (oc + 1) * 512], j == 0, j == 7)
            em.op("vector", "tensor_add", out=tmp[:, oc * 512:(oc + 1) * 512],
                  in0=ps_a[:, :], in1=bo_bc[:, oc * 512:(oc + 1) * 512])
        em.op("vector", "tensor_add", out=tmp[:], in0=tmp[:], in1=x_sb[:])
        em.op("sync", "dma_start", dma=True, out=xo[i * 128:(i + 1) * 128, :], in_=tmp[:])
    return p.close()


def _stage_l1aa(X, inp):
    im = [dict(xs=_ct(X[c * TPC:(c + 1) * TPC]),
               wqT=_ct(inp["L1_attn_a_Wq"].T), wkT=_ct(inp["L1_attn_a_Wk"].T),
               wvT=_ct(inp["L1_attn_a_Wv"].T), bq=_ct(inp["L1_attn_a_bq"][None]),
               bk=_ct(inp["L1_attn_a_bk"][None]), bv=_ct(inp["L1_attn_a_bv"][None]),
               nq=_ct(inp["L1_attn_a_norm_q_w"][None]), nk=_ct(inp["L1_attn_a_norm_k_w"][None]),
               eye=_EYE, ones_row=_ONES) for c in range(NC)]
    r9 = _run("p9", build_p9, im)
    q = np.concatenate([np.asarray(r9[c]["q"]) for c in range(NC)], axis=0)
    k = np.concatenate([np.asarray(r9[c]["k"]) for c in range(NC)], axis=0)
    v = np.concatenate([np.asarray(r9[c]["v"]) for c in range(NC)], axis=0)
    qh = q.reshape(B, T, H16, HD)
    kh = k.reshape(B, T, H16, HD)
    vh = v.reshape(B, T, H16, HD)
    att = np.zeros((B, T, H16, HD), np.float32)
    for rnd in range(4):
        im10 = []
        for c in range(NC):
            pair = rnd * NC + c
            b, h = pair // H16, pair % H16
            im10.append(dict(qT=_ct(qh[b, :, h, :].T), kT=_ct(kh[b, :, h, :].T),
                             vv=_ct(vh[b, :, h, :]), eye=_EYE, ones_row=_ONES))
        r10 = _run("p10", build_p10, im10)
        for c in range(NC):
            pair = rnd * NC + c
            b, h = pair // H16, pair % H16
            att[b, :, h, :] = np.asarray(r10[c]["o"])
    attf = att.reshape(NTOK, D)
    im11 = [dict(xs=_ct(X[c * TPC:(c + 1) * TPC]), att=_ct(attf[c * TPC:(c + 1) * TPC]),
                 woT=_ct(inp["L1_attn_a_Wo"].T), bo=_ct(inp["L1_attn_a_bo"][None]),
                 eye=_EYE, ones_row=_ONES) for c in range(NC)]
    r11 = _run("p11", build_p11, im11)
    return np.concatenate([np.asarray(r11[c]["xo"]) for c in range(NC)], axis=0)


def build_p12():
    p = P()
    xs_d = p.inp("xs", [TPC, D])
    wenc_d = p.inp("wenc", [D, 2048])
    bencT_d = p.inp("bencT", [128, 16])
    ws2c_d = p.inp("ws2c", [2048, 64])
    bs2c_d = p.inp("bs2c", [1, 64])
    wr_d = p.inp("wr", [64, 8])
    br_d = p.inp("br", [1, 8])
    w1_d = p.inp("w1", [8, 64, D])
    b1T_d = p.inp("b1T", [128, 64])
    w2_d = p.inp("w2", [8, D, 64])
    b2_d = p.inp("b2", [8, 64])
    wc2s_d = p.inp("wc2s", [64, 2048])
    bc2sT_d = p.inp("bc2sT", [128, 16])
    wdec_d = p.inp("wdec", [2048, D])
    bdec_d = p.inp("bdec", [1, D])
    lng_d = p.inp("lng", [1, D])
    lnb_d = p.inp("lnb", [1, D])
    eye_d = p.inp("eye", [128, 128])
    ones_d = p.inp("ones_row", [1, 128])
    xo = p.out("xo", [TPC, D])

    stage = p.sb("stage", [128, 1024])
    x_sb = p.sb("x_sb", [128, D])
    ln_sb = p.sb("ln_sb", [128, D])
    lnT_bf = p.sb("lnT_bf", [128, 8 * TPC], BF16)
    sq = p.sb("sq", [128, D])
    scr = p.sb("scr", [128, 4])
    eye_sb = p.sb("eye_sb", [128, 128])
    eye_bf = p.sb("eye_bf", [128, 128], BF16)
    wencb = p.sb("wencb", [128, 8 * 128], BF16)
    benc_c = p.sb("benc_c", [128, 16])
    ws2cb = p.sb("ws2cb", [128, 16 * 64], BF16)
    wrb = p.sb("wrb", [64, 8], BF16)
    w1b = p.sb("w1b", [64, D], BF16)
    b1c = p.sb("b1c", [128, 8 * 8])
    w2b = p.sb("w2b", [128, 8 * 8 * 64], BF16)
    wc2sb = p.sb("wc2sb", [64, 2048], BF16)
    bc2s_c = p.sb("bc2s_c", [128, 16])
    wdecb = p.sb("wdecb", [128, 16 * D], BF16)
    spkT = p.sb("spkT", [128, 16 * TPC], BF16)
    onesr = p.sb("onesr", [1, 128])
    bs2c_bc = p.sb("bs2c_bc", [128, 64])
    br_bc = p.sb("br_bc", [128, 8])
    bdec_bc = p.sb("bdec_bc", [128, D])
    lng_bc = p.sb("lng_bc", [128, D])
    lnb_bc = p.sb("lnb_bc", [128, D])
    b2_bc = p.sb("b2_bc", [128, 8 * 64])
    contT = p.sb("contT", [64, TPC], BF16)
    gate_sb = p.sb("gate_sb", [128, NT * 8])
    pr = p.sb("pr", [128, 8])
    pr2 = p.sb("pr2", [128, 8])
    msk = p.sb("msk", [128, 8])
    hT_bf = p.sb("hT_bf", [128, 4 * TPC], BF16)
    yacc = p.sb("yacc", [128, NT * 64])
    acc_eo = p.sb("acc_eo", [128, NT * 64])
    t1 = p.sb("t1", [128, 128])
    ps_a = p.ps("ps_a", [128, 512])
    ps_c = p.ps("ps_c", [128, 512])
    ps_tr = p.ps("ps_tr", [128, 128])
    ps_trb = p.ps("ps_trb", [128, 128], BF16)

    em = p.start()
    em.op("sync", "dma_start", dma=True, out=eye_sb[:], in_=eye_d)
    em.op("vector", "tensor_copy", out=eye_bf[:], in_=eye_sb[:])
    em.op("sync", "dma_start", dma=True, out=onesr[:], in_=ones_d)
    ones_row = onesr[0:1, :]
    for dd, dst, w in [(bs2c_d, bs2c_bc, 64), (br_d, br_bc, 8), (bdec_d, bdec_bc, D),
                       (lng_d, lng_bc, D), (lnb_d, lnb_bc, D)]:
        em.op("sync", "dma_start", dma=True, out=stage[0:1, 0:w], in_=dd)
        _bcast_row(em, ps_a, stage[0:1, 0:w], dst, ones_row, w)
    for e in range(8):
        em.op("sync", "dma_start", dma=True, out=stage[0:1, 0:64], in_=b2_d[e:e + 1, :])
        _bcast_row(em, ps_a, stage[0:1, 0:64], b2_bc[:, e * 64:(e + 1) * 64], ones_row, 64)
    em.op("sync", "dma_start", dma=True, out=benc_c[:], in_=bencT_d)
    em.op("sync", "dma_start", dma=True, out=bc2s_c[:], in_=bc2sT_d)
    em.op("sync", "dma_start", dma=True, out=b1c[:], in_=b1T_d)
    em.op("sync", "dma_start", dma=True, out=stage[0:64, 0:8], in_=wr_d)
    em.op("vector", "tensor_copy", out=wrb[:], in_=stage[0:64, 0:8])
    _load_cast(em, ws2c_d, stage, ws2cb, 2048, 64)
    for hf in range(2):
        em.op("sync", "dma_start", dma=True, out=stage[0:64, 0:1024], in_=wc2s_d[:, hf * 1024:(hf + 1) * 1024])
        em.op("vector", "tensor_copy", out=wc2sb[:, hf * 1024:(hf + 1) * 1024], in_=stage[0:64, 0:1024])
    _load_cast(em, wdec_d, stage, wdecb, 2048, D)
    for e in range(8):
        for j in range(8):
            em.op("sync", "dma_start", dma=True, out=stage[0:128, 0:64], in_=w2_d[e, j * 128:(j + 1) * 128, :])
            em.op("vector", "tensor_copy", out=w2b[:, (e * 8 + j) * 64:(e * 8 + j + 1) * 64], in_=stage[0:128, 0:64])
    for i in range(NT):
        em.op("sync", "dma_start", dma=True, out=x_sb[:], in_=xs_d[i * 128:(i + 1) * 128, :])
        _ln_tile(em, ln_sb[:], x_sb[:], sq[:], scr, 1e-5)
        for j in range(8):
            _tr(em, lnT_bf[:, j * TPC + i * 128:j * TPC + (i + 1) * 128],
                ln_sb[:, j * 128:(j + 1) * 128], ps_tr, eye_sb)
    for db in range(16):
        for j in range(8):
            em.op("sync", "dma_start", dma=True, out=stage[:, 0:128],
                  in_=wenc_d[j * 128:(j + 1) * 128, db * 128:(db + 1) * 128])
            em.op("vector", "tensor_copy", out=wencb[:, j * 128:(j + 1) * 128], in_=stage[:, 0:128])
        for tc in range(4):
            for j in range(8):
                em.mm(ps_a[:, :], wencb[:, j * 128:(j + 1) * 128],
                      lnT_bf[:, j * TPC + tc * 512:j * TPC + (tc + 1) * 512], j == 0, j == 7)
            em.op("scalar", "activation", out=spkT[:, db * TPC + tc * 512:db * TPC + (tc + 1) * 512],
                  in_=ps_a[:, :], func=AF.Sigmoid, bias=benc_c[:, db:db + 1])
    for i in range(NT):
        for j in range(16):
            em.mm(ps_a[:, 0:64], spkT[:, j * TPC + i * 128:j * TPC + (i + 1) * 128],
                  ws2cb[:, j * 64:(j + 1) * 64], j == 0, j == 15)
        em.op("vector", "tensor_add", out=t1[:, 0:64], in0=ps_a[:, 0:64], in1=bs2c_bc[:])
        _tr(em, contT[:, i * 128:(i + 1) * 128], t1[0:128, 0:64], ps_tr, eye_sb)
        em.mm(ps_a[:, 0:8], contT[:, i * 128:(i + 1) * 128], wrb[:], True, True)
        em.op("vector", "tensor_add", out=pr[:], in0=ps_a[:, 0:8], in1=br_bc[:])
        em.op("vector", "reduce_max", out=scr[:, 0:1], in_=pr[:], axis=AX.X)
        em.op("vector", "tensor_scalar", out=pr[:], in0=pr[:], scalar1=scr[:, 0:1],
              scalar2=None, op0=ALU.subtract)
        em.op("scalar", "activation", out=pr[:], in_=pr[:], func=AF.Exp)
        em.op("vector", "reduce_sum", out=scr[:, 0:1], in_=pr[:], axis=AX.X)
        em.op("vector", "reciprocal", out=scr[:, 0:1], in_=scr[:, 0:1])
        em.op("vector", "tensor_scalar", out=pr[:], in0=pr[:], scalar1=scr[:, 0:1],
              scalar2=None, op0=ALU.mult)
        em.op("vector", "reduce_max", out=scr[:, 1:2], in_=pr[:], axis=AX.X)
        em.op("vector", "tensor_scalar", out=msk[:], in0=pr[:], scalar1=scr[:, 1:2],
              scalar2=None, op0=ALU.is_ge)
        em.op("vector", "tensor_scalar", out=pr2[:], in0=msk[:], scalar1=-1.0, scalar2=1.0,
              op0=ALU.mult, op1=ALU.add)
        em.op("vector", "tensor_mul", out=pr2[:], in0=pr2[:], in1=pr[:])
        em.op("vector", "reduce_max", out=scr[:, 2:3], in_=pr2[:], axis=AX.X)
        em.op("vector", "tensor_scalar", out=pr2[:], in0=pr2[:], scalar1=scr[:, 2:3],
              scalar2=None, op0=ALU.is_ge)
        em.op("vector", "tensor_add", out=msk[:], in0=msk[:], in1=pr2[:])
        em.op("vector", "tensor_mul", out=msk[:], in0=msk[:], in1=pr[:])
        em.op("vector", "tensor_add", out=scr[:, 1:2], in0=scr[:, 1:2], in1=scr[:, 2:3])
        em.op("vector", "reciprocal", out=scr[:, 1:2], in_=scr[:, 1:2])
        em.op("vector", "tensor_scalar", out=gate_sb[:, i * 8:(i + 1) * 8], in0=msk[:],
              scalar1=scr[:, 1:2], scalar2=None, op0=ALU.mult)
        em.op("vector", "memset", acc_eo[:, i * 64:(i + 1) * 64], 0.0)
    for e in range(8):
        em.op("sync", "dma_start", dma=True, out=stage[0:64, 0:D], in_=w1_d[e])
        em.op("vector", "tensor_copy", out=w1b[:], in_=stage[0:64, 0:D])
        for dbg in range(2):
            for db in range(4):
                gdb = dbg * 4 + db
                for tc in range(4):
                    em.mm(ps_a[:, :], w1b[:, gdb * 128:(gdb + 1) * 128],
                          contT[:, tc * 512:(tc + 1) * 512], True, True)
                    em.op("scalar", "activation", out=hT_bf[:, db * TPC + tc * 512:db * TPC + (tc + 1) * 512],
                          in_=ps_a[:, :], func=AF.Relu, bias=b1c[:, gdb * 8 + e:gdb * 8 + e + 1])
            for i in range(NT):
                for db in range(4):
                    gdb = dbg * 4 + db
                    em.mm(ps_c[:, 0:64], hT_bf[:, db * TPC + i * 128:db * TPC + (i + 1) * 128],
                          w2b[:, (e * 8 + gdb) * 64:(e * 8 + gdb + 1) * 64], db == 0, db == 3)
                if dbg == 0:
                    em.op("vector", "tensor_copy", out=yacc[:, i * 64:(i + 1) * 64], in_=ps_c[:, 0:64])
                else:
                    em.op("vector", "tensor_add", out=yacc[:, i * 64:(i + 1) * 64],
                          in0=yacc[:, i * 64:(i + 1) * 64], in1=ps_c[:, 0:64])
        for i in range(NT):
            em.op("vector", "tensor_add", out=t1[:, 0:64], in0=yacc[:, i * 64:(i + 1) * 64], in1=b2_bc[:, e * 64:(e + 1) * 64])
            em.op("vector", "tensor_scalar", out=t1[:, 64:128], in0=t1[:, 0:64],
                  scalar1=gate_sb[:, i * 8 + e:i * 8 + e + 1], scalar2=None, op0=ALU.mult)
            em.op("vector", "tensor_add", out=acc_eo[:, i * 64:(i + 1) * 64],
                  in0=acc_eo[:, i * 64:(i + 1) * 64], in1=t1[:, 64:128])
    eoT_bf = contT
    for i in range(NT):
        _tr(em, eoT_bf[:, i * 128:(i + 1) * 128], acc_eo[:, i * 64:(i + 1) * 64], ps_tr, eye_sb)
    for db in range(16):
        for tc in range(4):
            em.mm(ps_a[:, :], wc2sb[:, db * 128:(db + 1) * 128],
                  eoT_bf[:, tc * 512:(tc + 1) * 512], True, True)
            em.op("scalar", "activation", out=spkT[:, db * TPC + tc * 512:db * TPC + (tc + 1) * 512],
                  in_=ps_a[:, :], func=AF.Sigmoid, bias=bc2s_c[:, db:db + 1])
    for i in range(NT):
        em.op("sync", "dma_start", dma=True, out=x_sb[:], in_=xs_d[i * 128:(i + 1) * 128, :])
        for oc in range(2):
            for j in range(16):
                em.mm(ps_a[:, :], spkT[:, j * TPC + i * 128:j * TPC + (i + 1) * 128],
                      wdecb[:, j * D + oc * 512:j * D + (oc + 1) * 512], j == 0, j == 15)
            em.op("vector", "tensor_add", out=ln_sb[:, oc * 512:(oc + 1) * 512],
                  in0=ps_a[:, :], in1=bdec_bc[:, oc * 512:(oc + 1) * 512])
        _ln_tile(em, ln_sb[:], ln_sb[:], sq[:], scr, 1e-5)
        em.op("vector", "tensor_mul", out=ln_sb[:], in0=ln_sb[:], in1=lng_bc[:])
        em.op("vector", "tensor_add", out=ln_sb[:], in0=ln_sb[:], in1=lnb_bc[:])
        em.op("vector", "tensor_add", out=ln_sb[:], in0=ln_sb[:], in1=x_sb[:])
        em.op("sync", "dma_start", dma=True, out=xo[i * 128:(i + 1) * 128, :], in_=ln_sb[:])
    return p.close()


def _stage_l1ffn(X, inp):
    im = [dict(xs=_ct(X[c * TPC:(c + 1) * TPC]),
               wenc=_ct(inp["L1_ffn_W_enc"]), bencT=_ct(inp["L1_ffn_b_enc"].reshape(16, 128).T),
               ws2c=_ct(inp["L1_ffn_W_s2c"]), bs2c=_ct(inp["L1_ffn_b_s2c"][None]),
               wr=_ct(inp["L1_ffn_Wr"]), br=_ct(inp["L1_ffn_br"][None]),
               w1=_ct(inp["L1_ffn_W1"]), b1T=_ct(inp["L1_ffn_b1"].reshape(8, 8, 128).transpose(2, 1, 0).reshape(128, 64)),
               w2=_ct(inp["L1_ffn_W2"]), b2=_ct(inp["L1_ffn_b2"]),
               wc2s=_ct(inp["L1_ffn_W_c2s"]), bc2sT=_ct(inp["L1_ffn_b_c2s"].reshape(16, 128).T),
               wdec=_ct(inp["L1_ffn_W_dec"]), bdec=_ct(inp["L1_ffn_b_dec"][None]),
               lng=_ct(inp["L1_ffn_ln_g"][None]), lnb=_ct(inp["L1_ffn_ln_b"][None]),
               eye=_EYE, ones_row=_ONES) for c in range(NC)]
    r = _run("p12", build_p12, im)
    return np.concatenate([np.asarray(r[c]["xo"]) for c in range(NC)], axis=0)


def build_p13():
    p = P()
    xs_d = p.inp("xs", [TPC, D])
    wkT_d = p.inp("wkT", [D, D])
    wvT_d = p.inp("wvT", [D, D])
    wrT_d = p.inp("wrT", [D, D])
    lng_d = p.inp("lng", [1, D])
    lnb_d = p.inp("lnb", [1, D])
    eye_d = p.inp("eye", [128, 128])
    ones_d = p.inp("ones_row", [1, 128])
    kv_o = p.out("kv", [TPC, D])
    r_o = p.out("r", [TPC, D])

    x_sb = p.sb("x_sb", [128, D])
    stage = p.sb("stage", [128, D])
    wk_sb = p.sb("wk_sb", [128, 8 * D], BF16)
    wv_sb = p.sb("wv_sb", [128, 8 * D], BF16)
    wr_sb = p.sb("wr_sb", [128, 8 * D], BF16)
    onesr = p.sb("onesr", [1, 128])
    lng_bc = p.sb("lng_bc", [128, D])
    lnb_bc = p.sb("lnb_bc", [128, D])
    eye_sb = p.sb("eye_sb", [128, 128])
    ln_sb = p.sb("ln_sb", [128, D])
    lnT_bf = p.sb("lnT_bf", [128, 8 * 128], BF16)
    sq = p.sb("sq", [128, D])
    scr = p.sb("scr", [128, 4])
    kbuf = p.sb("kbuf", [128, D])
    t0 = p.sb("t0", [128, D])
    ps_a = p.ps("ps_a", [128, 512])
    ps_tr = p.ps("ps_tr", [128, 128])

    em = p.start()
    em.op("sync", "dma_start", dma=True, out=eye_sb[:], in_=eye_d)
    em.op("sync", "dma_start", dma=True, out=onesr[:], in_=ones_d)
    _load_cast(em, wkT_d, stage, wk_sb, D, D)
    _load_cast(em, wvT_d, stage, wv_sb, D, D)
    _load_cast(em, wrT_d, stage, wr_sb, D, D)
    for dd, dst in [(lng_d, lng_bc), (lnb_d, lnb_bc)]:
        em.op("sync", "dma_start", dma=True, out=stage[0:1, 0:D], in_=dd)
        _bcast_row(em, ps_a, stage[0:1, 0:D], dst, onesr[0:1, :], D)
    for i in range(NT):
        em.op("sync", "dma_start", dma=True, out=x_sb[:], in_=xs_d[i * 128:(i + 1) * 128, :])
        _ln_tile(em, ln_sb[:], x_sb[:], sq[:], scr, 1e-5)
        em.op("vector", "tensor_mul", out=ln_sb[:], in0=ln_sb[:], in1=lng_bc[:])
        em.op("vector", "tensor_add", out=ln_sb[:], in0=ln_sb[:], in1=lnb_bc[:])
        for j in range(8):
            _tr(em, lnT_bf[:, j * 128:(j + 1) * 128], ln_sb[:, j * 128:(j + 1) * 128], ps_tr, eye_sb)
        for w_sb, dst in [(wk_sb, kbuf), (wv_sb, t0), (wr_sb, None)]:
            for oc in range(2):
                for j in range(8):
                    em.mm(ps_a[:, :], lnT_bf[:, j * 128:(j + 1) * 128],
                          w_sb[:, j * D + oc * 512:j * D + (oc + 1) * 512], j == 0, j == 7)
                if dst is None:
                    em.op("vector", "tensor_copy", out=sq[:, oc * 512:(oc + 1) * 512], in_=ps_a[:, :])
                else:
                    em.op("vector", "tensor_copy", out=dst[:, oc * 512:(oc + 1) * 512], in_=ps_a[:, :])
        em.op("sync", "dma_start", dma=True, out=r_o[i * 128:(i + 1) * 128, :], in_=sq[:])
        em.op("vector", "tensor_mul", out=t0[:], in0=t0[:], in1=kbuf[:])
        em.op("sync", "dma_start", dma=True, out=kv_o[i * 128:(i + 1) * 128, :], in_=t0[:])
    return p.close()


def build_p14():
    p = P()
    kvT_d = p.inp("kvT", [128, B * T])
    dec_d = p.inp("dec", [128, 1])
    fir_d = p.inp("fir", [128, 1])
    tio_d = p.inp("tio", [128, T])
    wkvT_o = p.out("wkvT", [128, B * T])

    kvT = p.sb("kvT_sb", [128, T])
    tio = p.sb("tio_sb", [128, T])
    wp = p.sb("wp", [128, T])
    cs = p.sb("cs", [128, T])
    sh = p.sb("sh", [128, T])
    cols = p.sb("cols", [128, 4])
    em = p.start()
    em.op("sync", "dma_start", dma=True, out=tio[:], in_=tio_d)
    em.op("sync", "dma_start", dma=True, out=cols[:, 0:1], in_=dec_d)
    em.op("sync", "dma_start", dma=True, out=cols[:, 1:2], in_=fir_d)
    em.op("scalar", "activation", out=cols[:, 0:1], in_=cols[:, 0:1], func=AF.Sigmoid)
    em.op("scalar", "activation", out=cols[:, 1:2], in_=cols[:, 1:2], func=AF.Sigmoid)
    em.op("scalar", "activation", out=cols[:, 2:3], in_=cols[:, 0:1], func=AF.Ln)
    em.op("vector", "memset", cols[:, 3:4], 1.0)
    for b in range(B):
        em.op("sync", "dma_start", dma=True, out=kvT[:], in_=kvT_d[:, b * T:(b + 1) * T])
        em.op("vector", "tensor_scalar", out=wp[:], in0=tio[:], scalar1=cols[:, 2:3],
              scalar2=None, op0=ALU.mult)
        em.op("scalar", "activation", out=wp[:], in_=wp[:], func=AF.Exp)
        em.op("vector", "tensor_mul", out=wp[:], in0=wp[:], in1=kvT[:])
        em.op("vector", "tensor_tensor_scan", out=cs[:], data0=wp[:],
              data1=cols[:, 3:4].to_broadcast([128, T]), initial=0.0,
              op0=ALU.add, op1=ALU.mult)
        em.op("vector", "memset", sh[:, 0:1], 0.0)
        em.op("vector", "tensor_copy", out=sh[:, 1:T], in_=cs[:, 0:T - 1])
        em.op("vector", "tensor_scalar", out=kvT[:], in0=kvT[:], scalar1=cols[:, 1:2],
              scalar2=None, op0=ALU.mult)
        em.op("vector", "tensor_scalar", out=sh[:], in0=sh[:], scalar1=cols[:, 0:1],
              scalar2=None, op0=ALU.mult)
        em.op("vector", "tensor_add", out=kvT[:], in0=kvT[:], in1=sh[:])
        em.op("sync", "dma_start", dma=True, out=wkvT_o[:, b * T:(b + 1) * T], in_=kvT[:])
    return p.close()


def build_p15():
    p = P()
    xs_d = p.inp("xs", [TPC, D])
    r_d = p.inp("r", [TPC, D])
    wkv_d = p.inp("wkv", [TPC, D])
    woT_d = p.inp("woT", [D, D])
    eye_d = p.inp("eye", [128, 128])
    xo = p.out("xo", [TPC, D])

    x_sb = p.sb("x_sb", [128, D])
    r_sb = p.sb("r_sb", [128, D])
    w_sb = p.sb("w_sb", [128, D])
    stage = p.sb("stage", [128, D])
    wo_sb = p.sb("wo_sb", [128, 8 * D], BF16)
    eye_sb = p.sb("eye_sb", [128, 128])
    zT_bf = p.sb("zT_bf", [128, 8 * 128], BF16)
    tmp = p.sb("tmp", [128, D])
    ps_a = p.ps("ps_a", [128, 512])
    ps_tr = p.ps("ps_tr", [128, 128])

    em = p.start()
    em.op("sync", "dma_start", dma=True, out=eye_sb[:], in_=eye_d)
    _load_cast(em, woT_d, stage, wo_sb, D, D)
    for i in range(NT):
        em.op("sync", "dma_start", dma=True, out=x_sb[:], in_=xs_d[i * 128:(i + 1) * 128, :])
        em.op("sync", "dma_start", dma=True, out=r_sb[:], in_=r_d[i * 128:(i + 1) * 128, :])
        em.op("sync", "dma_start", dma=True, out=w_sb[:], in_=wkv_d[i * 128:(i + 1) * 128, :])
        em.op("scalar", "activation", out=r_sb[:], in_=r_sb[:], func=AF.Sigmoid)
        em.op("vector", "tensor_mul", out=r_sb[:], in0=r_sb[:], in1=w_sb[:])
        for j in range(8):
            _tr(em, zT_bf[:, j * 128:(j + 1) * 128], r_sb[:, j * 128:(j + 1) * 128], ps_tr, eye_sb)
        for oc in range(2):
            for j in range(8):
                em.mm(ps_a[:, :], zT_bf[:, j * 128:(j + 1) * 128],
                      wo_sb[:, j * D + oc * 512:j * D + (oc + 1) * 512], j == 0, j == 7)
            em.op("vector", "tensor_copy", out=tmp[:, oc * 512:(oc + 1) * 512], in_=ps_a[:, :])
        em.op("vector", "tensor_add", out=tmp[:], in0=tmp[:], in1=x_sb[:])
        em.op("sync", "dma_start", dma=True, out=xo[i * 128:(i + 1) * 128, :], in_=tmp[:])
    return p.close()


def _stage_l1ab(X, inp):
    im = [dict(xs=_ct(X[c * TPC:(c + 1) * TPC]),
               wkT=_ct(inp["L1_attn_b_Wk"].T), wvT=_ct(inp["L1_attn_b_Wv"].T),
               wrT=_ct(inp["L1_attn_b_Wr"].T),
               lng=_ct(inp["L1_attn_b_ln_gamma"][None]), lnb=_ct(inp["L1_attn_b_ln_beta"][None]),
               eye=_EYE, ones_row=_ONES) for c in range(NC)]
    r13 = _run("p13", build_p13, im)
    kv = np.concatenate([np.asarray(r13[c]["kv"]) for c in range(NC)], axis=0)
    rr = np.concatenate([np.asarray(r13[c]["r"]) for c in range(NC)], axis=0)
    kvT = np.ascontiguousarray(kv.reshape(B, T, D).transpose(2, 0, 1).reshape(D, B * T))
    tio = np.tile(np.arange(T, dtype=np.float32), (128, 1))
    im14 = [dict(kvT=_ct(kvT[c * 128:(c + 1) * 128]),
                 dec=_ct(inp["L1_attn_b_time_decay"][c * 128:(c + 1) * 128][:, None]),
                 fir=_ct(inp["L1_attn_b_time_first"][c * 128:(c + 1) * 128][:, None]),
                 tio=tio) for c in range(NC)]
    r14 = _run("p14", build_p14, im14)
    wkvT = np.concatenate([np.asarray(r14[c]["wkvT"]) for c in range(NC)], axis=0)
    wkv = np.ascontiguousarray(wkvT.reshape(D, B, T).transpose(1, 2, 0).reshape(NTOK, D))
    im15 = [dict(xs=_ct(X[c * TPC:(c + 1) * TPC]), r=_ct(rr[c * TPC:(c + 1) * TPC]),
                 wkv=_ct(wkv[c * TPC:(c + 1) * TPC]), woT=_ct(inp["L1_attn_b_Wo"].T),
                 eye=_EYE) for c in range(NC)]
    r15 = _run("p15", build_p15, im15)
    return np.concatenate([np.asarray(r15[c]["xo"]) for c in range(NC)], axis=0)


def build_p6():
    p = P()
    xs_d = p.inp("xs", [TPC, D])
    xp_d = p.inp("xp", [TPC, D])
    mux_d = p.inp("mux", [1, D])
    at_d = p.inp("at", [D, 192])
    ab_d = p.inp("ab", [1, 192])
    bbd_d = p.inp("bbd", [192, 6 * D])
    lamb_d = p.inp("lamb", [1, 6 * D])
    bbrow_d = p.inp("bbrow", [1, 6 * D])
    mbd_d = p.inp("mbd", [D, 5 * 128])
    gw_d = p.inp("gw", [D, D])
    wabd_d = p.inp("wabd", [D, 128])
    wbbd_d = p.inp("wbbd", [D, 128])
    wlam_d = p.inp("wlam", [1, D])
    eye_d = p.inp("eye", [128, 128])
    ones_d = p.inp("ones_row", [1, 128])
    k_o = p.out("ko", [TPC, D])
    v_o = p.out("vo", [TPC, D])
    r_o = p.out("ro", [TPC, D])
    u_o = p.out("uo", [TPC, D])
    lw_o = p.out("lwo", [TPC, D])
    g_o = p.out("go", [TPC, D])

    stage = p.sb("stage", [128, 2048])
    at_bf = p.sb("at_bf", [128, 8 * 192], BF16)
    bbd_bf = p.sb("bbd_bf", [128, 2 * 6 * D], BF16)
    mbd_bf = p.sb("mbd_bf", [128, 8 * 5 * 128], BF16)
    gw_bf = p.sb("gw_bf", [128, 8 * D], BF16)

    onesr = p.sb("onesr", [1, 128])
    rowbuf = p.sb("rowbuf", [1, 6 * D])
    mux_bc = p.sb("mux_bc", [128, D])
    ab_bc = p.sb("ab_bc", [128, 192])
    lb_bc = p.sb("lb_bc", [128, 6 * D])
    eye_sb = p.sb("eye_sb", [128, 128])
    eye_bf = p.sb("eye_bf", [128, 128], BF16)
    ln_sb = p.sb("ln_sb", [128, D])
    sq = p.sb("sq", [128, D])
    scr = p.sb("scr", [128, 4])
    diff = p.sb("diff", [128, D])
    z_sb = p.sb("z_sb", [128, D])
    zT_bf = p.sb("zT_bf", [128, 8 * 128], BF16)
    hdn = p.sb("hdn", [128, 192])
    hdnT_bf = p.sb("hdnT_bf", [128, 2 * 128], BF16)
    lora = p.sb("lora", [128, 6 * D])
    ddT_bf = p.sb("ddT_bf", [128, 48 * 128], BF16)
    t0 = p.sb("t0", [128, D])
    t1 = p.sb("t1", [128, D])
    ps_a = p.ps("ps_a", [128, 512])
    ps_tr = p.ps("ps_tr", [128, 128])
    ps_trb = p.ps("ps_trb", [128, 128], BF16)

    em = p.start()
    em.op("sync", "dma_start", dma=True, out=eye_sb[:], in_=eye_d)
    em.op("vector", "tensor_copy", out=eye_bf[:], in_=eye_sb[:])
    em.op("sync", "dma_start", dma=True, out=onesr[:], in_=ones_d)
    em.op("vector", "memset", bbd_bf[:, 6 * D:12 * D], 0.0)
    em.op("vector", "memset", hdnT_bf[:, 128:256], 0.0)
    _load_cast(em, at_d, stage, at_bf, D, 192)
    for j in range(2):
        nr = 128 if j == 0 else 64
        for hf in range(3):
            em.op("sync", "dma_start", dma=True, out=stage[0:nr, 0:2048],
                  in_=bbd_d[j * 128:j * 128 + nr, hf * 2048:(hf + 1) * 2048])
            em.op("vector", "tensor_copy",
                  out=bbd_bf[0:nr, j * 6 * D + hf * 2048:j * 6 * D + (hf + 1) * 2048],
                  in_=stage[0:nr, 0:2048])
    _load_cast(em, mbd_d, stage, mbd_bf, D, 5 * 128)
    _load_cast(em, gw_d, stage, gw_bf, D, D)

    for dd, dst, w in [(mux_d, mux_bc, D), (ab_d, ab_bc, 192)]:
        em.op("sync", "dma_start", dma=True, out=stage[0:1, 0:w], in_=dd)
        _bcast_row(em, ps_a, stage[0:1, 0:w], dst, onesr[0:1, :], w)
    em.op("sync", "dma_start", dma=True, out=rowbuf[0:1, :], in_=lamb_d)
    _bcast_row(em, ps_a, rowbuf[0:1, :], lb_bc, onesr[0:1, :], 6 * D)
    em.op("sync", "dma_start", dma=True, out=rowbuf[0:1, :], in_=bbrow_d)
    for oc in range(12):
        em.mm(ps_a[:, :], onesr[0:1, :], rowbuf[0:1, oc * 512:(oc + 1) * 512], True, True)
        em.op("vector", "tensor_add", out=lb_bc[:, oc * 512:(oc + 1) * 512],
              in0=lb_bc[:, oc * 512:(oc + 1) * 512], in1=ps_a[:, :])
    for i in range(NT):
        em.op("sync", "dma_start", dma=True, out=t0[:], in_=xs_d[i * 128:(i + 1) * 128, :])
        _ln_tile(em, ln_sb[:], t0[:], sq[:], scr, 1e-5)
        em.op("sync", "dma_start", dma=True, out=t0[:], in_=xp_d[i * 128:(i + 1) * 128, :])
        _ln_tile(em, diff[:], t0[:], sq[:], scr, 1e-5)
        em.op("vector", "tensor_sub", out=diff[:], in0=diff[:], in1=ln_sb[:])
        em.op("vector", "tensor_mul", out=t0[:], in0=diff[:], in1=mux_bc[:])
        em.op("vector", "tensor_add", out=t0[:], in0=t0[:], in1=ln_sb[:])
        em.op("vector", "tensor_mul", out=t0[:], in0=t0[:], in1=diff[:])
        em.op("vector", "tensor_add", out=z_sb[:], in0=t0[:], in1=ln_sb[:])
        for j in range(8):
            _tr(em, zT_bf[:, j * 128:(j + 1) * 128], z_sb[:, j * 128:(j + 1) * 128], ps_tr, eye_sb)
        for j in range(8):
            em.mm(ps_a[:, 0:192], zT_bf[:, j * 128:(j + 1) * 128],
                  at_bf[:, j * 192:(j + 1) * 192], j == 0, j == 7)
        em.op("vector", "tensor_add", out=hdn[:, 0:192], in0=ps_a[:, 0:192], in1=ab_bc[:])
        em.op("scalar", "activation", out=hdn[:, 0:192], in_=hdn[:, 0:192], func=AF.Tanh)
        _tr(em, hdnT_bf[:, 0:128], hdn[:, 0:128], ps_tr, eye_sb)
        _tr(em, hdnT_bf[0:64, 128:256], hdn[:, 128:192], ps_tr, eye_sb)
        for oc in range(12):
            em.mm(ps_a[:, :], hdnT_bf[:, 0:128], bbd_bf[:, oc * 512:(oc + 1) * 512], True, False)
            em.mm(ps_a[:, :], hdnT_bf[:, 128:256], bbd_bf[:, 6 * D + oc * 512:6 * D + (oc + 1) * 512], False, True)
            em.op("vector", "tensor_add", out=lora[:, oc * 512:(oc + 1) * 512],
                  in0=ps_a[:, :], in1=lb_bc[:, oc * 512:(oc + 1) * 512])
        for j6 in range(6):
            em.op("vector", "tensor_mul", out=t0[:], in0=diff[:], in1=lora[:, j6 * D:(j6 + 1) * D])
            em.op("vector", "tensor_add", out=t0[:], in0=t0[:], in1=ln_sb[:])
            for j in range(8):
                _tr(em, ddT_bf[:, (j6 * 8 + j) * 128:(j6 * 8 + j + 1) * 128],
                    t0[:, j * 128:(j + 1) * 128], ps_tr, eye_sb)
        import os as _os
        for j6 in (range(5) if _os.environ.get("P6SKIP","") not in ("proj",) else []):
            dst = t0 if j6 == 3 else t1
            for j in range(8):
                em.mm(ps_a[:, 0:128], ddT_bf[:, (j6 * 8 + j) * 128:(j6 * 8 + j + 1) * 128],
                      mbd_bf[:, j * 5 * 128 + j6 * 128:j * 5 * 128 + (j6 + 1) * 128], True, True)
                em.op("vector", "tensor_copy", out=dst[:, j * 128:(j + 1) * 128], in_=ps_a[:, 0:128])
            if j6 != 3:
                out_d = [k_o, v_o, r_o, None, u_o][j6]
                em.op("sync", "dma_start", dma=True, out=out_d[i * 128:(i + 1) * 128, :], in_=t1[:])
        for oc in (range(2) if _os.environ.get("P6SKIP","") not in ("proj", "keepproj", "keepgate2") else []):
            for j in range(8):
                em.mm(ps_a[:, :], ddT_bf[:, (5 * 8 + j) * 128:(5 * 8 + j + 1) * 128],
                      gw_bf[:, j * D + oc * 512:j * D + (oc + 1) * 512], j == 0, j == 7)
            em.op("vector", "tensor_copy", out=t1[:, oc * 512:(oc + 1) * 512], in_=ps_a[:, :])
        em.op("sync", "dma_start", dma=True, out=g_o[i * 128:(i + 1) * 128, :], in_=t1[:])
        em.op("sync", "dma_start", dma=True, out=lw_o[i * 128:(i + 1) * 128, :], in_=t0[:])
    return p.close()


def build_p7():
    p = P()
    kT_d = p.inp("kT", [HD, T])
    vT_d = p.inp("vT", [HD, T])
    rT_d = p.inp("rT", [HD, T])
    uT_d = p.inp("uT", [HD, T])
    wT_d = p.inp("wT", [HD, T])
    wa_d = p.inp("wa", [HD, HD])
    wb_d = p.inp("wb", [HD, HD])
    wlam_d = p.inp("wlamc", [HD, 1])
    ones_d = p.inp("ones_row", [1, 128])
    o_o = p.out("o", [HD, T])

    kT = p.sb("kT_s", [64, T])
    wdec = p.sb("wdec", [64, T])
    S = p.sb("S_s", [64, T])
    bvec = p.sb("bvec", [64, T])
    acc = p.sb("acc", [64, T])
    wh_bf = p.sb("wh_bf", [64, T], BF16)
    wt_bf = p.sb("wt_bf", [64, T], BF16)
    row_st = p.sb("row_st", [1, 2048])
    onesr_bf = p.sb("onesr_bf", [1, 64], BF16)
    onec_bf = p.sb("onec_bf", [64, 1], BF16)
    ab_bf = p.sb("ab_bf", [64, 128], BF16)
    lamc = p.sb("lamc", [64, 1])
    ps_b = p.ps("ps_b", [64, 512])
    ps_w = p.ps("ps_w", [64, 512])
    ps_s = p.ps("ps_s", [1, 512])

    em = p.start()
    row_bf = wt_bf[0:1, :]
    em.op("vector", "memset", onesr_bf[:], 1.0)
    em.op("vector", "memset", onec_bf[:], 1.0)
    em.op("sync", "dma_start", dma=True, out=row_st[0:1, 0:64].rearrange("a b -> b a") if False else lamc[:], in_=wlam_d)
    em.op("sync", "dma_start", dma=True, out=S[:, 0:64], in_=wa_d)
    em.op("vector", "tensor_copy", out=ab_bf[:, 0:64], in_=S[:, 0:64])
    em.op("sync", "dma_start", dma=True, out=S[:, 0:64], in_=wb_d)
    em.op("vector", "tensor_copy", out=ab_bf[:, 64:128], in_=S[:, 0:64])
    em.op("sync", "dma_start", dma=True, out=S[:], in_=wT_d)
    em.op("vector", "tensor_copy", out=wt_bf[:], in_=S[:])
    for cc in range(16):
        em.mm(ps_w[:], ab_bf[:, 0:64], wt_bf[:, cc * 512:(cc + 1) * 512], True, True)
        em.op("scalar", "activation", out=wh_bf[:, cc * 512:(cc + 1) * 512], in_=ps_w[:], func=AF.Tanh)
    for cc in range(16):
        em.mm(ps_w[:], ab_bf[:, 64:128], wh_bf[:, cc * 512:(cc + 1) * 512], True, True)
        em.op("vector", "tensor_scalar", out=wdec[:, cc * 512:(cc + 1) * 512], in0=ps_w[:],
              scalar1=lamc[:, 0:1], scalar2=None, op0=ALU.add)
    em.op("scalar", "activation", out=wdec[:], in_=wdec[:], func=AF.Exp)
    em.op("vector", "tensor_scalar_mul", out=wdec[:], in0=wdec[:], scalar1=-1.0)
    em.op("scalar", "activation", out=wdec[:], in_=wdec[:], func=AF.Exp)
    em.op("sync", "dma_start", dma=True, out=kT[:], in_=kT_d)
    em.op("sync", "dma_start", dma=True, out=S[:], in_=uT_d)
    em.op("sync", "dma_start", dma=True, out=bvec[:], in_=vT_d)
    em.op("sync", "dma_start", dma=True, out=acc[:], in_=rT_d)
    em.op("vector", "tensor_mul", out=bvec[:], in0=bvec[:], in1=acc[:])
    em.op("vector", "tensor_mul", out=S[:], in0=S[:], in1=kT[:])
    vr_bf = wh_bf[0:1, :]
    for cc in range(8):
        em.op("vector", "tensor_copy", out=wt_bf[:, 0:1024], in_=bvec[:, cc * 1024:(cc + 1) * 1024])
        for h in range(2):
            em.mm(ps_s[:], onec_bf[:], wt_bf[:, h * 512:(h + 1) * 512], True, True)
            em.op("vector", "tensor_copy", out=vr_bf[0:1, cc * 1024 + h * 512:cc * 1024 + (h + 1) * 512], in_=ps_s[:])
    for cc in range(16):
        em.mm(ps_b[:], onesr_bf[0:1, 0:64], vr_bf[0:1, cc * 512:(cc + 1) * 512], True, True)
        em.op("vector", "tensor_mul", out=acc[:, cc * 512:(cc + 1) * 512],
              in0=ps_b[:], in1=S[:, cc * 512:(cc + 1) * 512])
    for e in range(64):
        for cc in range(4):
            em.op("sync", "dma_start", dma=True, out=row_st[0:1, :], in_=vT_d[e:e + 1, cc * 2048:(cc + 1) * 2048])
            em.op("vector", "tensor_copy", out=row_bf[0:1, cc * 2048:(cc + 1) * 2048], in_=row_st[0:1, :])
        for cc in range(16):
            em.mm(ps_b[:], onesr_bf[0:1, 0:64], row_bf[0:1, cc * 512:(cc + 1) * 512], True, True)
            em.op("vector", "tensor_mul", out=bvec[:, cc * 512:(cc + 1) * 512],
                  in0=ps_b[:], in1=kT[:, cc * 512:(cc + 1) * 512])
        em.op("vector", "tensor_tensor_scan", out=S[:], data0=wdec[:], data1=bvec[:],
              initial=0.0, op0=ALU.mult, op1=ALU.add)
        for cc in range(4):
            em.op("sync", "dma_start", dma=True, out=row_st[0:1, :], in_=rT_d[e:e + 1, cc * 2048:(cc + 1) * 2048])
            em.op("vector", "tensor_copy", out=row_bf[0:1, cc * 2048:(cc + 1) * 2048], in_=row_st[0:1, :])
        for cc in range(16):
            em.mm(ps_b[:], onesr_bf[0:1, 0:64], row_bf[0:1, cc * 512:(cc + 1) * 512], True, True)
            em.op("vector", "tensor_copy", out=bvec[:, cc * 512:(cc + 1) * 512], in_=ps_b[:])
        em.op("vector", "tensor_mul", out=S[:, 0:T - 1], in0=S[:, 0:T - 1], in1=bvec[:, 1:T])
        em.op("vector", "tensor_add", out=acc[:, 1:T], in0=acc[:, 1:T], in1=S[:, 0:T - 1])
    em.op("sync", "dma_start", dma=True, out=o_o, in_=acc[:])
    return p.close()


def build_p8():
    p = P()
    xs_d = p.inp("xs", [TPC, D])
    rw_d = p.inp("rw", [TPC, D])
    gg_d = p.inp("gg", [TPC, D])
    gng_d = p.inp("gng", [1, D])
    gnb_d = p.inp("gnb", [1, D])
    woT_d = p.inp("woT", [D, D])
    bo_d = p.inp("bo", [1, D])
    eye_d = p.inp("eye", [128, 128])
    ones_d = p.inp("ones_row", [1, 128])
    xo = p.out("xo", [TPC, D])

    x_sb = p.sb("x_sb", [128, D])
    rw_sb = p.sb("rw_sb", [128, D])
    gg_sb = p.sb("gg_sb", [128, D])
    stage = p.sb("stage", [128, D])
    wo_sb = p.sb("wo_sb", [128, 8 * D], BF16)
    onesr = p.sb("onesr", [1, 128])
    gng_bc = p.sb("gng_bc", [128, D])
    gnb_bc = p.sb("gnb_bc", [128, D])
    bo_bc = p.sb("bo_bc", [128, D])
    eye_sb = p.sb("eye_sb", [128, 128])
    sq = p.sb("sq", [128, D])
    st16 = p.sb("st16", [128, 64])
    zT_bf = p.sb("zT_bf", [128, 8 * 128], BF16)
    tmp = p.sb("tmp", [128, D])
    ps_a = p.ps("ps_a", [128, 512])
    ps_tr = p.ps("ps_tr", [128, 128])

    em = p.start()
    em.op("sync", "dma_start", dma=True, out=eye_sb[:], in_=eye_d)
    em.op("sync", "dma_start", dma=True, out=onesr[:], in_=ones_d)
    _load_cast(em, woT_d, stage, wo_sb, D, D)
    for dd, dst in [(gng_d, gng_bc), (gnb_d, gnb_bc), (bo_d, bo_bc)]:
        em.op("sync", "dma_start", dma=True, out=stage[0:1, 0:D], in_=dd)
        _bcast_row(em, ps_a, stage[0:1, 0:D], dst, onesr[0:1, :], D)
    for i in range(NT):
        em.op("sync", "dma_start", dma=True, out=x_sb[:], in_=xs_d[i * 128:(i + 1) * 128, :])
        em.op("sync", "dma_start", dma=True, out=rw_sb[:], in_=rw_d[i * 128:(i + 1) * 128, :])
        em.op("sync", "dma_start", dma=True, out=gg_sb[:], in_=gg_d[i * 128:(i + 1) * 128, :])
        rw3 = rw_sb[:].rearrange("p (h d) -> p h d", d=64)
        em.op("vector", "tensor_reduce", out=st16[:, 0:16], in_=rw3, axis=AX.X, op=ALU.add)
        em.op("vector", "tensor_scalar_mul", out=st16[:, 0:16], in0=st16[:, 0:16], scalar1=1.0 / 64)
        em.op("vector", "tensor_tensor", out=rw3, in0=rw3,
              in1=st16[:, 0:16].unsqueeze(2).to_broadcast([128, 16, 64]), op=ALU.subtract)
        em.op("scalar", "activation", out=sq[:], in_=rw_sb[:], func=AF.Square)
        em.op("vector", "tensor_reduce", out=st16[:, 16:32], in_=sq[:].rearrange("p (h d) -> p h d", d=64), axis=AX.X, op=ALU.add)
        em.op("vector", "tensor_scalar", out=st16[:, 16:32], in0=st16[:, 16:32],
              scalar1=1.0 / 64, scalar2=1e-3, op0=ALU.mult, op1=ALU.add)
        em.op("scalar", "activation", out=st16[:, 16:32], in_=st16[:, 16:32], func=AF.Sqrt)
        em.op("vector", "reciprocal", out=st16[:, 16:32], in_=st16[:, 16:32])
        em.op("vector", "tensor_tensor", out=rw3, in0=rw3,
              in1=st16[:, 16:32].unsqueeze(2).to_broadcast([128, 16, 64]), op=ALU.mult)
        em.op("vector", "tensor_mul", out=rw_sb[:], in0=rw_sb[:], in1=gng_bc[:])
        em.op("vector", "tensor_add", out=rw_sb[:], in0=rw_sb[:], in1=gnb_bc[:])
        em.op("scalar", "activation", out=gg_sb[:], in_=gg_sb[:], func=AF.Silu)
        em.op("vector", "tensor_mul", out=rw_sb[:], in0=rw_sb[:], in1=gg_sb[:])
        for j in range(8):
            _tr(em, zT_bf[:, j * 128:(j + 1) * 128], rw_sb[:, j * 128:(j + 1) * 128], ps_tr, eye_sb)
        for oc in range(2):
            for j in range(8):
                em.mm(ps_a[:, :], zT_bf[:, j * 128:(j + 1) * 128],
                      wo_sb[:, j * D + oc * 512:j * D + (oc + 1) * 512], j == 0, j == 7)
            em.op("vector", "tensor_add", out=tmp[:, oc * 512:(oc + 1) * 512],
                  in0=ps_a[:, :], in1=bo_bc[:, oc * 512:(oc + 1) * 512])
        em.op("vector", "tensor_add", out=tmp[:], in0=tmp[:], in1=x_sb[:])
        em.op("sync", "dma_start", dma=True, out=xo[i * 128:(i + 1) * 128, :], in_=tmp[:])
    return p.close()


def _stage_l0ab(X, inp):
    Xp = np.zeros_like(X)
    Xp[1:] = X[:-1]
    Xp[T] = 0.0
    Xp[0] = 0.0
    A = inp["L0_attn_b_lora_A_w"]
    at = np.concatenate([A[g] for g in range(6)], axis=1)
    Bw = inp["L0_attn_b_lora_B_w"]
    bbd = np.zeros((192, 6 * D), np.float32)
    for g in range(6):
        bbd[g * 32:(g + 1) * 32, g * D:(g + 1) * D] = Bw[g]
    mhd = inp["L0_attn_b_mhd_w"]
    mbd = np.zeros((D, 5 * 128), np.float32)
    for g in range(5):
        for h in range(16):
            j = h // 2
            co = (h % 2) * 64
            mbd[h * 64:(h + 1) * 64, g * 128 + co:g * 128 + co + 64] = mhd[g, h]
    wA = inp["L0_attn_b_wkv_A"]
    wB = inp["L0_attn_b_wkv_B"]
    wabd = np.zeros((D, 128), np.float32)
    wbbd = np.zeros((D, 128), np.float32)
    for h in range(16):
        co = (h % 2) * 64
        wabd[h * 64:(h + 1) * 64, co:co + 64] = wA[h]
        wbbd[h * 64:(h + 1) * 64, co:co + 64] = wB[h]
    im = [dict(xs=_ct(X[c * TPC:(c + 1) * TPC]), xp=_ct(Xp[c * TPC:(c + 1) * TPC]),
               mux=_ct(inp["L0_attn_b_mu_x"][None]), at=_ct(at),
               ab=_ct(inp["L0_attn_b_lora_A_b"].reshape(1, 192)),
               bbd=_ct(bbd), lamb=_ct(inp["L0_attn_b_lora_lambda"].reshape(1, 6 * D)),
               bbrow=_ct(inp["L0_attn_b_lora_B_b"].reshape(1, 6 * D)),
               mbd=_ct(mbd), gw=_ct(inp["L0_attn_b_gate_w"]),
               wabd=_ct(wabd), wbbd=_ct(wbbd),
               wlam=_ct(inp["L0_attn_b_wkv_lamb"].reshape(1, D)),
               eye=_EYE, ones_row=_ONES) for c in range(NC)]
    r6 = _run("p6", build_p6, im)
    gath = {}
    for nm in ["ko", "vo", "ro", "uo", "lwo", "go"]:
        gath[nm] = np.concatenate([np.asarray(r6[c][nm]) for c in range(NC)], axis=0)
    kh = gath["ko"].reshape(B, T, H16, HD)
    vh = gath["vo"].reshape(B, T, H16, HD)
    rh = gath["ro"].reshape(B, T, H16, HD)
    uh = gath["uo"].reshape(B, T, H16, HD)
    lh = gath["lwo"].reshape(B, T, H16, HD)
    rw = np.zeros((B, T, H16, HD), np.float32)
    for rnd in range(4):
        im7 = []
        for c in range(NC):
            pair = rnd * NC + c
            b, h = pair // H16, pair % H16
            im7.append(dict(kT=_ct(kh[b, :, h, :].T), vT=_ct(vh[b, :, h, :].T),
                            rT=_ct(rh[b, :, h, :].T), uT=_ct(uh[b, :, h, :].T),
                            wT=_ct(lh[b, :, h, :].T), wa=_ct(wA[h]), wb=_ct(wB[h]),
                            wlamc=_ct(inp["L0_attn_b_wkv_lamb"][h][:, None]),
                            ones_row=_ONES))
        r7 = _run("p7", build_p7, im7)
        for c in range(NC):
            pair = rnd * NC + c
            b, h = pair // H16, pair % H16
            rw[b, :, h, :] = np.asarray(r7[c]["o"]).T
    rwf = rw.reshape(NTOK, D)
    im8 = [dict(xs=_ct(X[c * TPC:(c + 1) * TPC]), rw=_ct(rwf[c * TPC:(c + 1) * TPC]),
                gg=_ct(gath["go"][c * TPC:(c + 1) * TPC]),
                gng=_ct(inp["L0_attn_b_gn_gamma"][None]), gnb=_ct(inp["L0_attn_b_gn_beta"][None]),
                woT=_ct(inp["L0_attn_b_out_w"].T) if False else _ct(inp["L0_attn_b_out_w"]),
                bo=_ct(inp["L0_attn_b_out_b"][None]), eye=_EYE, ones_row=_ONES)
           for c in range(NC)]
    r8 = _run("p8", build_p8, im8)
    return np.concatenate([np.asarray(r8[c]["xo"]) for c in range(NC)], axis=0)
